# version 01
# baseline (speedup 1.0000x reference)
import contextlib
import numpy as np
import ml_dtypes
import concourse.bass as bass
import concourse.mybir as mybir
from concourse.bass_utils import run_bass_kernel_spmd
from concourse.bass_core_rust import IndirectOffsetOnAxis

F32 = mybir.dt.float32
BF16 = mybir.dt.bfloat16
I32 = mybir.dt.int32
AF = mybir.ActivationFunctionType
ALU = mybir.AluOpType
AX = mybir.AxisListType
BF_NP = ml_dtypes.bfloat16

NCORES = 8
BATCH, SEQ, DM = 4, 8192, 1024
TLOC = SEQ // 2


_PROG_CACHE = {}


def cached(name, fn, *args):
    key = (name,) + args
    if key not in _PROG_CACHE:
        _PROG_CACHE[key] = fn(*args)
    return _PROG_CACHE[key]


class Sched:

    def __init__(self):
        self.phases = []

    def dma(self, fn, n, queue="sync"):
        self.phases.append((queue, fn, 16 * n))

    def comp(self, engine, fn):
        self.phases.append((engine, fn, 1))

    def emit(self, nc, block, chain):
        starts, v = [], 0
        for (_, _, n) in self.phases:
            starts.append(v)
            v += n
        total = v
        phases, ph_starts = self.phases, starts

        def run_engine(name, eng):
            for (e, fn, n), stt in zip(phases, ph_starts):
                if e == name:
                    eng.wait_ge(chain, stt)
                    fn(eng)
            if name == "sync":
                eng.wait_ge(chain, total)

        @block.sync
        def _(eng):
            run_engine("sync", eng)

        @block.vector
        def _(eng):
            run_engine("vector", eng)

        @block.scalar
        def _(eng):
            run_engine("scalar", eng)

        @block.tensor
        def _(eng):
            run_engine("tensor", eng)

        @block.gpsimd
        def _(eng):
            run_engine("gpsimd", eng)
        return total


def mkphases(sch, chain):
    def dph(pairs, queue="sync"):
        pairs = list(pairs)

        def fn(eng):
            for dst, src in pairs:
                eng.dma_start(dst, src).then_inc(chain, 16)
        sch.dma(fn, len(pairs), queue)

    def gph(fn_emit):
        sch.dma(fn_emit, 1, "gpsimd")

    def cph(engine, body):
        def fn(eng):
            body(eng).then_inc(chain, 1)
        sch.comp(engine, fn)
    return dph, gph, cph


def emit_ln(dph, cph, sx, sout, smu, ssq, srstd, sscr, ntiles, H, eps_ap=None):
    def f_sq(eng):
        for j in range(ntiles):
            last = eng.activation(sscr[:, 0:H], sx[:, j * H:(j + 1) * H],
                                  AF.Square, accum_out=ssq[:, j:j + 1])
        return last
    cph("scalar", f_sq)

    def f_su(eng):
        for j in range(ntiles):
            last = eng.reduce_sum(smu[:, j:j + 1], sx[:, j * H:(j + 1) * H], AX.X)
        return last
    cph("vector", f_su)

    def f_sc(eng):
        eng.tensor_scalar(smu[:, 0:ntiles], smu[:, 0:ntiles], 1.0 / H, None, ALU.mult)
        return eng.tensor_scalar(ssq[:, 0:ntiles], ssq[:, 0:ntiles], 1.0 / H, None, ALU.mult)
    cph("vector", f_sc)

    def f_mu2(eng):
        return eng.tensor_tensor(srstd[:, 0:ntiles], smu[:, 0:ntiles], smu[:, 0:ntiles], ALU.mult)
    cph("vector", f_mu2)

    def f_var(eng):
        return eng.tensor_tensor(srstd[:, 0:ntiles], ssq[:, 0:ntiles], srstd[:, 0:ntiles], ALU.subtract)
    cph("vector", f_var)

    def f_sqrt(eng):
        return eng.activation(srstd[:, 0:ntiles], srstd[:, 0:ntiles], AF.Sqrt, bias=eps_ap)
    cph("scalar", f_sqrt)

    def f_rcp(eng):
        return eng.reciprocal(srstd[:, 0:ntiles], srstd[:, 0:ntiles])
    cph("vector", f_rcp)

    def f_sub(eng):
        for j in range(ntiles):
            last = eng.tensor_scalar(sout[:, j * H:(j + 1) * H], sx[:, j * H:(j + 1) * H],
                                     smu[:, j:j + 1], None, ALU.subtract)
        return last
    cph("vector", f_sub)

    def f_mul(eng):
        for j in range(ntiles):
            last = eng.tensor_scalar(sout[:, j * H:(j + 1) * H], sout[:, j * H:(j + 1) * H],
                                     srstd[:, j:j + 1], None, ALU.mult)
        return last
    cph("vector", f_mul)


def build_l0ffn(ntok):
    nc = bass.Bass("TRN2", target_bir_lowering=False)
    TG = 256
    NG = ntok // TG
    NTT = TG // 128
    H, F = 1024, 2048
    NH = H // 128

    xin = nc.dram_tensor("xin", [ntok, H], F32, kind="ExternalInput").ap()
    xlast = nc.dram_tensor("xlast", [8, 128], F32, kind="ExternalInput").ap()
    Wp1 = nc.dram_tensor("Wp1", [H, 128], F32, kind="ExternalInput").ap()
    bp1 = nc.dram_tensor("bp1", [128, 1], F32, kind="ExternalInput").ap()
    Wp2 = nc.dram_tensor("Wp2", [128, 8], F32, kind="ExternalInput").ap()
    bp2 = nc.dram_tensor("bp2", [8, 1], F32, kind="ExternalInput").ap()
    W1r = nc.dram_tensor("W1r", [8 * H, F], F32, kind="ExternalInput").ap()
    W2r = nc.dram_tensor("W2r", [8 * F, H], F32, kind="ExternalInput").ap()
    b1r = nc.dram_tensor("b1r", [8, F], F32, kind="ExternalInput").ap()
    b2r = nc.dram_tensor("b2r", [8, H], F32, kind="ExternalInput").ap()
    Wo = nc.dram_tensor("Wo", [H, H], F32, kind="ExternalInput").ap()
    bo_row = nc.dram_tensor("bo_row", [1, H], F32, kind="ExternalInput").ap()
    identf = nc.dram_tensor("identf", [128, 128], F32, kind="ExternalInput").ap()
    ones_row = nc.dram_tensor("ones_row", [1, TG], F32, kind="ExternalInput").ap()
    onescol128 = nc.dram_tensor("onescol128", [1, 128], F32, kind="ExternalInput").ap()
    iota128 = nc.dram_tensor("iota128", [128, 1], F32, kind="ExternalInput").ap()
    iota8 = nc.dram_tensor("iota8", [8, 1], F32, kind="ExternalInput").ap()
    revbig = nc.dram_tensor("revbig", [1, 8], F32, kind="ExternalInput").ap()

    xout = nc.dram_tensor("xout", [ntok, H], F32, kind="ExternalOutput").ap()
    o_sel = nc.dram_tensor("o_sel", [1, 1], F32, kind="ExternalOutput").ap()

    with contextlib.ExitStack() as st:
        def sb(name, shape, dt=F32):
            return st.enter_context(nc.sbuf_tensor(name, shape, dt))

        w1bf = sb("w1bf", [128, 8 * F], BF16)
        w2bf = sb("w2bf", [128, 16 * H], BF16)
        wobf = sb("wobf", [128, 8 * H], BF16)
        stag = sb("stag", [128, F], F32)
        b1row = sb("b1row", [1, F], F32)
        b2row = sb("b2row", [1, H], F32)
        borow = sb("borow", [1, H], F32)
        onesr = sb("onesr", [1, TG], F32)
        onesc = sb("onesc", [1, 128], F32)
        siota = sb("siota", [128, 1], F32)
        sident = sb("sident", [128, 128], F32)

        sxl = sb("sxl", [8, 128], F32)
        sxls = sb("sxls", [8, 128], F32)
        sones8 = sb("sones8", [8, 1], F32)
        sones18 = sb("sones18", [1, 8], F32)
        sstat = sb("sstat", [8, 2], F32)
        st11 = sb("st11", [1, 4], F32)
        swp1 = sb("swp1", [128, 8 * 128], F32)
        sbp1 = sb("sbp1", [128, 1], F32)
        swp2 = sb("swp2", [128, 8], F32)
        sbp2 = sb("sbp2", [8, 1], F32)
        sxlT = sb("sxlT", [128, 8], F32)
        sg1 = sb("sg1", [128, 1], F32)
        slogT = sb("slogT", [8, 1], F32)
        slog = sb("slog", [1, 8], F32)
        srb = sb("srb", [1, 8], F32)
        sm1 = sb("sm1", [1, 1], F32)
        sarg = sb("sarg", [1, 8], F32)
        ssel = sb("ssel", [1, 1], F32)
        sidxf = sb("sidxf", [128, 1], F32)
        sidx = sb("sidx", [128, 1], I32)
        sselrep = sb("sselrep", [128, 1], F32)
        seps = sb("seps", [128, 1], F32)
        siota8 = sb("siota8", [8, 1], F32)
        sonehot = sb("sonehot", [8, 1], F32)
        b1all = sb("b1all", [8, F], F32)
        b2all = sb("b2all", [8, H], F32)

        sx = sb("sx", [128, H * NTT], F32)
        shid = sb("shid", [128, H * NTT], F32)
        smu = sb("smu", [128, NTT], F32)
        ssq = sb("ssq", [128, NTT], F32)
        srstd = sb("srstd", [128, NTT], F32)
        sscr = sb("sscr", [128, H], F32)
        shTb = sb("shTb", [128, 8 * TG], BF16)
        sh1 = sb("sh1", [128, 16 * TG], BF16)
        sr1 = sb("sr1", [128, 8 * TG], BF16)
        so1 = sb("so1", [128, 8 * TG], F32)

        pt = st.enter_context(nc.psum_tensor("pt", [128, 1024], F32))
        pmm = [st.enter_context(nc.psum_tensor(f"pmm{q}", [128, TG], F32)) for q in range(4)]
        psmall = st.enter_context(nc.psum_tensor("psmall", [128, 64], F32))

        chain = st.enter_context(nc.semaphore("chain"))
        block = st.enter_context(nc.Block())
        sch = Sched()
        dph, gph, cph = mkphases(sch, chain)

        dph([(sxl[:], xlast), (swp1[:], Wp1), (sbp1[:], bp1), (swp2[:], Wp2),
             (sbp2[:], bp2), (borow[:], bo_row), (onesr[:], ones_row),
             (onesc[:], onescol128), (siota[:], iota128), (sident[:], identf),
             (srb[:], revbig), (siota8[:], iota8), (b1all[:], b1r), (b2all[:], b2r)])
        cph("vector", lambda eng: eng.memset(sones8[:], 1.0))
        cph("vector", lambda eng: eng.memset(sones18[:], 1.0))
        cph("vector", lambda eng: eng.memset(seps[:], 1e-5))

        cph("scalar", lambda eng: eng.activation(sxls[:], sxl[:], AF.Square,
                                                 accum_out=sstat[:, 1:2]))
        cph("vector", lambda eng: eng.reduce_sum(sstat[:, 0:1], sxl[:], AX.X))
        cph("tensor", lambda eng: eng.matmul(psmall[0:1, 0:2], sones8[:], sstat[:],
                                             start=True, stop=True))
        cph("vector", lambda eng: eng.tensor_scalar(st11[:, 0:2], psmall[0:1, 0:2],
                                                    1.0 / 1024.0, None, ALU.mult))
        cph("vector", lambda eng: eng.tensor_tensor(st11[:, 2:3], st11[:, 0:1],
                                                    st11[:, 0:1], ALU.mult))
        cph("vector", lambda eng: eng.tensor_tensor(st11[:, 2:3], st11[:, 1:2],
                                                    st11[:, 2:3], ALU.subtract))
        cph("scalar", lambda eng: eng.activation(st11[:, 2:3], st11[:, 2:3],
                                                 AF.Sqrt, bias=seps[0:1, 0:1]))
        cph("vector", lambda eng: eng.reciprocal(st11[:, 2:3], st11[:, 2:3]))
        cph("vector", lambda eng: eng.tensor_copy(st11[:, 1:2], st11[:, 2:3]))
        cph("tensor", lambda eng: eng.matmul(psmall[0:8, 2:4], sones18[:], st11[:, 0:2],
                                             start=True, stop=True))
        cph("vector", lambda eng: eng.tensor_scalar(sxls[:], sxl[:], psmall[0:8, 2:3],
                                                    None, ALU.subtract))
        cph("vector", lambda eng: eng.tensor_scalar(sxls[:], sxls[:], psmall[0:8, 3:4],
                                                    None, ALU.mult))
        cph("tensor", lambda eng: eng.transpose(psmall[:, 8:16], sxls[0:8, 0:128], sident[0:8, 0:8]))
        cph("scalar", lambda eng: eng.copy(sxlT[:], psmall[:, 8:16]))

        def f_mmp1(eng):
            for k in range(8):
                last = eng.matmul(psmall[:, 16:17], swp1[:, 128 * k:128 * (k + 1)],
                                  sxlT[:, k:k + 1], start=(k == 0), stop=(k == 7))
            return last
        cph("tensor", f_mmp1)
        cph("scalar", lambda eng: eng.activation(sg1[:], psmall[:, 16:17], AF.Gelu,
                                                 bias=sbp1[:, 0:1]))
        cph("tensor", lambda eng: eng.matmul(psmall[0:8, 17:18], swp2[:], sg1[:],
                                             start=True, stop=True))
        cph("vector", lambda eng: eng.tensor_scalar(slogT[:], psmall[0:8, 17:18],
                                                    sbp2[:, 0:1], None, ALU.add))
        cph("tensor", lambda eng: eng.transpose(psmall[0:1, 24:32], slogT[0:8, 0:1], sident[0:8, 0:8]))
        cph("scalar", lambda eng: eng.copy(slog[:], psmall[0:1, 24:32]))
        cph("vector", lambda eng: eng.reduce_max(sm1[:], slog[:], AX.X))
        cph("vector", lambda eng: eng.scalar_tensor_tensor(sarg[:], slog[:], sm1[0:1, 0:1],
                                                           srb[:], ALU.is_equal, ALU.mult))
        cph("vector", lambda eng: eng.reduce_max(sm1[:], sarg[:], AX.X))
        cph("vector", lambda eng: eng.tensor_scalar(ssel[:], sm1[:], -1.0, 100.0,
                                                    ALU.mult, ALU.add))
        dph([(o_sel, ssel[:])])
        cph("tensor", lambda eng: eng.matmul(psmall[:, 32:33], onesc[:], ssel[:],
                                             start=True, stop=True))
        cph("vector", lambda eng: eng.tensor_copy(sselrep[:], psmall[:, 32:33]))

        cph("vector", lambda eng: eng.scalar_tensor_tensor(sidxf[:], sselrep[:], 1024.0,
                                                           siota[:], ALU.mult, ALU.add))
        for c in range(8):
            cph("vector", lambda eng, c=c: eng.tensor_scalar(sidx[:], sidxf[:],
                                                             float(128 * c), None, ALU.add))
            gph(lambda eng, c=c: eng.indirect_dma_start(
                stag[:], None, W1r, IndirectOffsetOnAxis(sidx[:, 0:1], 0)).then_inc(chain, 16))
            cph("vector", lambda eng, c=c: eng.tensor_copy(
                w1bf[:, c * F:(c + 1) * F], stag[:]))
        cph("vector", lambda eng: eng.scalar_tensor_tensor(sidxf[:], sselrep[:], 2048.0,
                                                           siota[:], ALU.mult, ALU.add))
        for c in range(16):
            cph("vector", lambda eng, c=c: eng.tensor_scalar(sidx[:], sidxf[:],
                                                             float(128 * c), None, ALU.add))
            gph(lambda eng, c=c: eng.indirect_dma_start(
                stag[:, 0:H], None, W2r, IndirectOffsetOnAxis(sidx[:, 0:1], 0)).then_inc(chain, 16))
            cph("vector", lambda eng, c=c: eng.tensor_copy(
                w2bf[:, c * H:(c + 1) * H], stag[:, 0:H]))
        cph("tensor", lambda eng: eng.matmul(psmall[0:8, 18:19], sones18[:], ssel[:],
                                             start=True, stop=True))
        cph("vector", lambda eng: eng.tensor_tensor(sonehot[:], siota8[:],
                                                    psmall[0:8, 18:19], ALU.is_equal))
        for c in range(8):
            cph("tensor", lambda eng, c=c: eng.matmul(
                pmm[0][0:1, 0:256], sonehot[:], b1all[:, 256 * c:256 * (c + 1)],
                start=True, stop=True))
            cph("scalar", lambda eng, c=c: eng.copy(
                b1row[0:1, 256 * c:256 * (c + 1)], pmm[0][0:1, 0:256]))
        for c in range(4):
            cph("tensor", lambda eng, c=c: eng.matmul(
                pmm[0][0:1, 0:256], sonehot[:], b2all[:, 256 * c:256 * (c + 1)],
                start=True, stop=True))
            cph("scalar", lambda eng, c=c: eng.copy(
                b2row[0:1, 256 * c:256 * (c + 1)], pmm[0][0:1, 0:256]))
        for c in range(8):
            dph([(stag[:, 0:H], Wo[128 * c:128 * (c + 1), :])])
            cph("vector", lambda eng, c=c: eng.tensor_copy(
                wobf[:, c * H:(c + 1) * H], stag[:, 0:H]))

        for g in range(NG):
            t0 = g * TG
            dph([(sx[:, j * H:(j + 1) * H], xin[t0 + 128 * j:t0 + 128 * (j + 1), :])
                 for j in range(NTT)])
            emit_ln(dph, cph, sx, shid, smu, ssq, srstd, sscr, NTT, H, seps[:, 0:1])

            for hc in range(NH):
                def f_tp(eng, hc=hc):
                    for j in range(NTT):
                        last = eng.transpose(pt[:, 128 * j:128 * (j + 1)],
                                             shid[:, j * H + 128 * hc:j * H + 128 * (hc + 1)],
                                             sident[:])
                    return last
                cph("tensor", f_tp)
                cph("scalar", lambda eng, hc=hc: eng.copy(
                    shTb[:, hc * TG:(hc + 1) * TG], pt[:, 0:TG]))

            for fc0 in range(0, 16, 4):
                def f_mm(eng, fc0=fc0):
                    for q in range(4):
                        fc = fc0 + q
                        pm = pmm[q]
                        for k in range(NH):
                            eng.matmul(pm[:], w1bf[:, k * F + 128 * fc:k * F + 128 * (fc + 1)],
                                       shTb[:, k * TG:(k + 1) * TG],
                                       start=(k == 0), stop=False)
                        last = eng.matmul(pm[:], b1row[0:1, 128 * fc:128 * (fc + 1)],
                                          onesr[:], start=False, stop=True)
                    return last
                cph("tensor", f_mm)

                def f_ge(eng, fc0=fc0):
                    for q in range(4):
                        fc = fc0 + q
                        last = eng.activation(sh1[:, fc * TG:(fc + 1) * TG],
                                              pmm[q][:], AF.Gelu)
                    return last
                cph("scalar", f_ge)

            for mc0 in range(0, 8, 4):
                def f_mm(eng, mc0=mc0):
                    for q in range(4):
                        mc = mc0 + q
                        pm = pmm[q]
                        for k in range(16):
                            eng.matmul(pm[:], w2bf[:, k * H + 128 * mc:k * H + 128 * (mc + 1)],
                                       sh1[:, k * TG:(k + 1) * TG],
                                       start=(k == 0), stop=False)
                        last = eng.matmul(pm[:], b2row[0:1, 128 * mc:128 * (mc + 1)],
                                          onesr[:], start=False, stop=True)
                    return last
                cph("tensor", f_mm)

                def f_ev(eng, mc0=mc0):
                    for q in range(4):
                        mc = mc0 + q
                        last = eng.copy(sr1[:, mc * TG:(mc + 1) * TG], pmm[q][:])
                    return last
                cph("scalar", f_ev)

            for mc0 in range(0, 8, 4):
                def f_mm(eng, mc0=mc0):
                    for q in range(4):
                        mc = mc0 + q
                        pm = pmm[q]
                        for k in range(8):
                            eng.matmul(pm[:], wobf[:, k * H + 128 * mc:k * H + 128 * (mc + 1)],
                                       sr1[:, k * TG:(k + 1) * TG],
                                       start=(k == 0), stop=False)
                        last = eng.matmul(pm[:], borow[0:1, 128 * mc:128 * (mc + 1)],
                                          onesr[:], start=False, stop=True)
                    return last
                cph("tensor", f_mm)

                def f_ev(eng, mc0=mc0):
                    for q in range(4):
                        mc = mc0 + q
                        last = eng.copy(so1[:, mc * TG:(mc + 1) * TG], pmm[q][:])
                    return last
                cph("scalar", f_ev)

            for j in range(NTT):
                def f_tp(eng, j=j):
                    for hc in range(NH):
                        last = eng.transpose(pt[:, 128 * hc:128 * (hc + 1)],
                                             so1[:, hc * TG + 128 * j:hc * TG + 128 * (j + 1)],
                                             sident[:])
                    return last
                cph("tensor", f_tp)
                cph("vector", lambda eng, j=j: eng.scalar_tensor_tensor(
                    sscr[:, 0:H], pt[:, 0:H], 0.15, sx[:, j * H:(j + 1) * H],
                    ALU.mult, ALU.add))
                cph("vector", lambda eng, j=j: eng.scalar_tensor_tensor(
                    sx[:, j * H:(j + 1) * H], shid[:, j * H:(j + 1) * H], 0.5,
                    sscr[:, 0:H], ALU.mult, ALU.add))

            dph([(xout[t0 + 128 * j:t0 + 128 * (j + 1), :], sx[:, j * H:(j + 1) * H])
                 for j in range(NTT)])

        sch.emit(nc, block, chain)
    return nc


def l0ffn_inmaps(x_full, ins, ntok):
    H, F = 1024, 2048
    common = {
        "Wp1": np.concatenate([ins["L0_ffn_Wp1"][128 * k:128 * (k + 1), :]
                               for k in range(8)], axis=1),
        "bp1": ins["L0_ffn_bp1"].reshape(128, 1),
        "Wp2": ins["L0_ffn_Wp2"],
        "bp2": ins["L0_ffn_bp2"].reshape(8, 1),
        "W1r": ins["L0_ffn_W1"].reshape(8 * H, F),
        "W2r": ins["L0_ffn_W2"].reshape(8 * F, H),
        "b1r": ins["L0_ffn_b1"],
        "b2r": ins["L0_ffn_b2"],
        "Wo": ins["L0_ffn_Wo"],
        "bo_row": ins["L0_ffn_bo"].reshape(1, H),
        "identf": np.eye(128, dtype=np.float32),
        "ones_row": np.ones((1, 256), dtype=np.float32),
        "onescol128": np.ones((1, 128), dtype=np.float32),
        "iota128": np.arange(128, dtype=np.float32).reshape(128, 1),
        "revbig": (100.0 - np.arange(8, dtype=np.float32)).reshape(1, 8),
        "iota8": np.arange(8, dtype=np.float32).reshape(8, 1),
    }
    common = {k: np.ascontiguousarray(v, dtype=np.float32) for k, v in common.items()}
    maps = []
    for c in range(NCORES):
        b, h = c // 2, c % 2
        m = dict(common)
        m["xin"] = np.ascontiguousarray(x_full[b, h * ntok:(h + 1) * ntok, :])
        m["xlast"] = np.ascontiguousarray(x_full[b, SEQ - 1, :].reshape(8, 128))
        maps.append(m)
    return maps


def run_l0ffn(x_full, ins):
    nc = cached('build_l0ffn', build_l0ffn, TLOC)
    res = run_bass_kernel_spmd(nc, l0ffn_inmaps(x_full, ins, TLOC),
                               core_ids=list(range(NCORES)))
    out = np.empty((BATCH, SEQ, DM), np.float32)
    for c in range(NCORES):
        b, h = c // 2, c % 2
        out[b, h * TLOC:(h + 1) * TLOC, :] = res.results[c]["xout"]
    sels = [res.results[2 * b]["o_sel"][0, 0] for b in range(BATCH)]
    return out, sels


def build_l0attn_a(ntok):
    nc = bass.Bass("TRN2", target_bir_lowering=False)
    TG = 256
    NG = ntok // TG
    NTT = TG // 128
    H, J = 1024, 3072
    NH, NJ = H // 128, J // 128

    xin = nc.dram_tensor("xin", [ntok, H], F32, kind="ExternalInput").ap()
    WinT = nc.dram_tensor("WinT", [H, J], F32, kind="ExternalInput").ap()
    grep = nc.dram_tensor("grep", [128, H], F32, kind="ExternalInput").ap()
    brep = nc.dram_tensor("brep", [128, H], F32, kind="ExternalInput").ap()
    identf = nc.dram_tensor("identf", [128, 128], F32, kind="ExternalInput").ap()
    xz = nc.dram_tensor("xz", [J, ntok], F32, kind="ExternalOutput").ap()

    with contextlib.ExitStack() as st:
        def sb(name, shape, dt=F32):
            return st.enter_context(nc.sbuf_tensor(name, shape, dt))
        winbf = sb("winbf", [128, NH * J], BF16)
        stag = sb("stag", [128, J], F32)
        sgrep = sb("sgrep", [128, H], F32)
        sbrep = sb("sbrep", [128, H], F32)
        sident = sb("sident", [128, 128], F32)
        seps = sb("seps", [128, 1], F32)
        sx = sb("sx", [128, H * NTT], F32)
        shid = sb("shid", [128, H * NTT], F32)
        smu = sb("smu", [128, NTT], F32)
        ssq = sb("ssq", [128, NTT], F32)
        srstd = sb("srstd", [128, NTT], F32)
        sscr = sb("sscr", [128, H], F32)
        shTb = sb("shTb", [128, NH * TG], BF16)
        sxz = sb("sxz", [128, 4 * TG], F32)

        pt = st.enter_context(nc.psum_tensor("pt", [128, 512], F32))
        pmm = [st.enter_context(nc.psum_tensor(f"pmm{q}", [128, TG], F32)) for q in range(4)]

        chain = st.enter_context(nc.semaphore("chain"))
        block = st.enter_context(nc.Block())
        sch = Sched()
        dph, gph, cph = mkphases(sch, chain)

        dph([(sgrep[:], grep), (sbrep[:], brep), (sident[:], identf)])
        cph("vector", lambda eng: eng.memset(seps[:], 1e-5))
        for k in range(NH):
            dph([(stag[:], WinT[128 * k:128 * (k + 1), :])])
            cph("vector", lambda eng, k=k: eng.tensor_copy(winbf[:, k * J:(k + 1) * J], stag[:]))

        for g in range(NG):
            t0 = g * TG
            dph([(sx[:, j * H:(j + 1) * H], xin[t0 + 128 * j:t0 + 128 * (j + 1), :])
                 for j in range(NTT)])
            emit_ln(dph, cph, sx, shid, smu, ssq, srstd, sscr, NTT, H, seps[:, 0:1])

            def f_g(eng):
                for j in range(NTT):
                    last = eng.tensor_tensor(shid[:, j * H:(j + 1) * H],
                                             shid[:, j * H:(j + 1) * H], sgrep[:], ALU.mult)
                return last
            cph("vector", f_g)

            def f_b(eng):
                for j in range(NTT):
                    last = eng.tensor_tensor(shid[:, j * H:(j + 1) * H],
                                             shid[:, j * H:(j + 1) * H], sbrep[:], ALU.add)
                return last
            cph("vector", f_b)

            for hc in range(NH):
                def f_tp(eng, hc=hc):
                    for j in range(NTT):
                        last = eng.transpose(pt[:, 128 * j:128 * (j + 1)],
                                             shid[:, j * H + 128 * hc:j * H + 128 * (hc + 1)],
                                             sident[:])
                    return last
                cph("tensor", f_tp)
                cph("scalar", lambda eng, hc=hc: eng.copy(
                    shTb[:, hc * TG:(hc + 1) * TG], pt[:, 0:TG]))

            for jc0 in range(0, NJ, 4):
                def f_mm(eng, jc0=jc0):
                    for q in range(4):
                        jc = jc0 + q
                        for k in range(NH):
                            last = eng.matmul(pmm[q][:],
                                              winbf[:, k * J + 128 * jc:k * J + 128 * (jc + 1)],
                                              shTb[:, k * TG:(k + 1) * TG],
                                              start=(k == 0), stop=(k == NH - 1))
                    return last
                cph("tensor", f_mm)

                def f_ev(eng, jc0=jc0):
                    for q in range(4):
                        last = eng.tensor_scalar(sxz[:, q * TG:(q + 1) * TG], pmm[q][:],
                                                 -5.0, 5.0, ALU.max, ALU.min)
                    return last
                cph("vector", f_ev)
                dph([(xz[128 * (jc0 + q):128 * (jc0 + q + 1), t0:t0 + TG],
                      sxz[:, q * TG:(q + 1) * TG]) for q in range(4)])

        sch.emit(nc, block, chain)
    return nc


def build_l0attn_b(iloc, L):
    nc = bass.Bass("TRN2", target_bir_lowering=False)
    NI = iloc // 128
    D = 8
    TC = 1024
    NTC = L // TC

    xs = nc.dram_tensor("xs", [iloc, L], F32, kind="ExternalInput").ap()
    gate = nc.dram_tensor("gate", [iloc, L], F32, kind="ExternalInput").ap()
    alog = nc.dram_tensor("alog", [iloc, D], F32, kind="ExternalInput").ap()
    bmat = nc.dram_tensor("bmat", [iloc, D], F32, kind="ExternalInput").ap()
    cmat = nc.dram_tensor("cmat", [iloc, D], F32, kind="ExternalInput").ap()
    sy_out = nc.dram_tensor("sy", [iloc, L], F32, kind="ExternalOutput").ap()

    with contextlib.ExitStack() as st:
        def sb(name, shape, dt=F32):
            return st.enter_context(nc.sbuf_tensor(name, shape, dt))
        sdec = sb("sdec", [128, NI * D], F32)
        sb01 = sb("sb01", [128, NI * D], F32)
        scc = sb("scc", [128, NI * D], F32)
        sstate = sb("sstate", [128, NI * D], F32)
        sxs = sb("sxs", [128, NI * TC], F32)
        sgt = sb("sgt", [128, NI * TC], F32)
        sy = sb("syt", [128, NI * TC], F32)
        sh = sb("sh", [128, NI * TC], F32)
        su = sb("su", [128, NI * TC], F32)

        chain = st.enter_context(nc.semaphore("chain"))
        block = st.enter_context(nc.Block())
        sch = Sched()
        dph, gph, cph = mkphases(sch, chain)

        dph([(sdec[:, it * D:(it + 1) * D], alog[128 * it:128 * (it + 1), :])
             for it in range(NI)])
        dph([(sb01[:, it * D:(it + 1) * D], bmat[128 * it:128 * (it + 1), :])
             for it in range(NI)])
        dph([(scc[:, it * D:(it + 1) * D], cmat[128 * it:128 * (it + 1), :])
             for it in range(NI)])
        cph("vector", lambda eng: eng.tensor_scalar(sdec[:], sdec[:], -5.0, 0.0,
                                                    ALU.max, ALU.min))
        cph("scalar", lambda eng: eng.activation(sdec[:], sdec[:], AF.Exp))
        cph("vector", lambda eng: eng.tensor_scalar(sdec[:], sdec[:], -1.0, None, ALU.mult))
        cph("vector", lambda eng: eng.tensor_scalar(sdec[:], sdec[:], -2.0, -0.01,
                                                    ALU.max, ALU.min))
        cph("vector", lambda eng: eng.tensor_scalar(sdec[:], sdec[:], 0.9, None, ALU.mult))
        cph("vector", lambda eng: eng.tensor_scalar(sb01[:], sb01[:], 0.1, None, ALU.mult))
        cph("vector", lambda eng: eng.memset(sstate[:], 0.0))

        for tc in range(NTC):
            c0 = tc * TC
            dph([(sxs[:, it * TC:(it + 1) * TC],
                  xs[128 * it:128 * (it + 1), c0:c0 + TC]) for it in range(NI)])
            dph([(sgt[:, it * TC:(it + 1) * TC],
                  gate[128 * it:128 * (it + 1), c0:c0 + TC]) for it in range(NI)])
            cph("vector", lambda eng: eng.memset(sy[:], 0.0))
            for d in range(D):
                def f_u(eng, d=d):
                    for it in range(NI):
                        last = eng.tensor_scalar(su[:, it * TC:(it + 1) * TC],
                                                 sxs[:, it * TC:(it + 1) * TC],
                                                 sb01[:, it * D + d:it * D + d + 1],
                                                 None, ALU.mult)
                    return last
                cph("vector", f_u)

                def f_scan(eng, d=d):
                    for it in range(NI):
                        last = eng.tensor_tensor_scan(
                            sh[:, it * TC:(it + 1) * TC],
                            sdec[:, it * D + d:it * D + d + 1].broadcast_to([128, TC]),
                            su[:, it * TC:(it + 1) * TC],
                            sstate[:, it * D + d:it * D + d + 1],
                            ALU.mult, ALU.add)
                    return last
                cph("vector", f_scan)

                def f_acc(eng, d=d):
                    for it in range(NI):
                        eng.tensor_copy(sstate[:, it * D + d:it * D + d + 1],
                                        sh[:, (it + 1) * TC - 1:(it + 1) * TC])
                        last = eng.scalar_tensor_tensor(
                            sy[:, it * TC:(it + 1) * TC],
                            sh[:, it * TC:(it + 1) * TC],
                            scc[:, it * D + d:it * D + d + 1],
                            sy[:, it * TC:(it + 1) * TC], ALU.mult, ALU.add)
                    return last
                cph("vector", f_acc)

            cph("vector", lambda eng: eng.tensor_scalar(sy[:], sy[:], -5.0, 5.0,
                                                        ALU.max, ALU.min))
            cph("scalar", lambda eng: eng.activation(sgt[:], sgt[:], AF.Tanh))
            cph("vector", lambda eng: eng.tensor_tensor(sy[:], sy[:], sgt[:], ALU.mult))
            dph([(sy_out[128 * it:128 * (it + 1), c0:c0 + TC],
                  sy[:, it * TC:(it + 1) * TC]) for it in range(NI)])

        sch.emit(nc, block, chain)
    return nc


def build_l0attn_c(ntok):
    nc = bass.Bass("TRN2", target_bir_lowering=False)
    TG = 256
    NG = ntok // TG
    NTT = TG // 128
    H, I = 1024, 1536
    NH, NIc = H // 128, I // 128

    syin = nc.dram_tensor("syin", [I, ntok], F32, kind="ExternalInput").ap()
    WoutT = nc.dram_tensor("WoutT", [I, H], F32, kind="ExternalInput").ap()
    x1in = nc.dram_tensor("x1in", [ntok, H], F32, kind="ExternalInput").ap()
    grep = nc.dram_tensor("grep", [128, H], F32, kind="ExternalInput").ap()
    brep = nc.dram_tensor("brep", [128, H], F32, kind="ExternalInput").ap()
    identf = nc.dram_tensor("identf", [128, 128], F32, kind="ExternalInput").ap()
    x2out = nc.dram_tensor("x2out", [ntok, H], F32, kind="ExternalOutput").ap()

    with contextlib.ExitStack() as st:
        def sb(name, shape, dt=F32):
            return st.enter_context(nc.sbuf_tensor(name, shape, dt))
        woutbf = sb("woutbf", [128, NIc * H], BF16)
        stag = sb("stag", [128, H], F32)
        sgrep = sb("sgrep", [128, H], F32)
        sbrep = sb("sbrep", [128, H], F32)
        sident = sb("sident", [128, 128], F32)
        seps = sb("seps", [128, 1], F32)
        ssy = sb("ssy", [128, NIc * TG], F32)
        ssyb = sb("ssyb", [128, NIc * TG], BF16)
        sot = sb("sot", [128, NH * TG], F32)
        sx = sb("sx", [128, H * NTT], F32)
        sat = sb("sat", [128, H * NTT], F32)
        smu = sb("smu", [128, NTT], F32)
        ssq = sb("ssq", [128, NTT], F32)
        srstd = sb("srstd", [128, NTT], F32)
        sscr = sb("sscr", [128, H], F32)

        pt = st.enter_context(nc.psum_tensor("pt", [128, 1024], F32))
        pmm = [st.enter_context(nc.psum_tensor(f"pmm{q}", [128, TG], F32)) for q in range(4)]

        chain = st.enter_context(nc.semaphore("chain"))
        block = st.enter_context(nc.Block())
        sch = Sched()
        dph, gph, cph = mkphases(sch, chain)

        dph([(sgrep[:], grep), (sbrep[:], brep), (sident[:], identf)])
        cph("vector", lambda eng: eng.memset(seps[:], 1e-5))
        for k in range(NIc):
            dph([(stag[:], WoutT[128 * k:128 * (k + 1), :])])
            cph("vector", lambda eng, k=k: eng.tensor_copy(woutbf[:, k * H:(k + 1) * H], stag[:]))

        for g in range(NG):
            t0 = g * TG
            dph([(ssy[:, k * TG:(k + 1) * TG], syin[128 * k:128 * (k + 1), t0:t0 + TG])
                 for k in range(NIc)])
            cph("vector", lambda eng: eng.tensor_copy(ssyb[:], ssy[:]))

            for mc0 in range(0, NH, 4):
                def f_mm(eng, mc0=mc0):
                    for q in range(4):
                        mc = mc0 + q
                        for k in range(NIc):
                            last = eng.matmul(pmm[q][:],
                                              woutbf[:, k * H + 128 * mc:k * H + 128 * (mc + 1)],
                                              ssyb[:, k * TG:(k + 1) * TG],
                                              start=(k == 0), stop=(k == NIc - 1))
                    return last
                cph("tensor", f_mm)

                def f_ev(eng, mc0=mc0):
                    for q in range(4):
                        mc = mc0 + q
                        last = eng.tensor_scalar(sot[:, mc * TG:(mc + 1) * TG], pmm[q][:],
                                                 -5.0, 5.0, ALU.max, ALU.min)
                    return last
                cph("vector", f_ev)

            for j in range(NTT):
                def f_tp(eng, j=j):
                    for hc in range(NH):
                        last = eng.transpose(pt[:, 128 * hc:128 * (hc + 1)],
                                             sot[:, hc * TG + 128 * j:hc * TG + 128 * (j + 1)],
                                             sident[:])
                    return last
                cph("tensor", f_tp)
                cph("scalar", lambda eng, j=j: eng.copy(sat[:, j * H:(j + 1) * H], pt[:, 0:H]))

            emit_ln(dph, cph, sat, sat, smu, ssq, srstd, sscr, NTT, H, seps[:, 0:1])
            dph([(sx[:, j * H:(j + 1) * H], x1in[t0 + 128 * j:t0 + 128 * (j + 1), :])
                 for j in range(NTT)])

            def f_g(eng):
                for j in range(NTT):
                    last = eng.tensor_tensor(sat[:, j * H:(j + 1) * H],
                                             sat[:, j * H:(j + 1) * H], sgrep[:], ALU.mult)
                return last
            cph("vector", f_g)

            def f_b(eng):
                for j in range(NTT):
                    last = eng.tensor_tensor(sat[:, j * H:(j + 1) * H],
                                             sat[:, j * H:(j + 1) * H], sbrep[:], ALU.add)
                return last
            cph("vector", f_b)

            def f_res(eng):
                for j in range(NTT):
                    last = eng.tensor_tensor(sx[:, j * H:(j + 1) * H],
                                             sx[:, j * H:(j + 1) * H],
                                             sat[:, j * H:(j + 1) * H], ALU.add)
                return last
            cph("vector", f_res)
            dph([(x2out[t0 + 128 * j:t0 + 128 * (j + 1), :], sx[:, j * H:(j + 1) * H])
                 for j in range(NTT)])

        sch.emit(nc, block, chain)
    return nc


def run_l0attn(x1_full, ins):
    H, I2, I = 1024, 3072, 1536
    nc = cached('build_l0attn_a', build_l0attn_a, TLOC)
    common = {
        "WinT": np.ascontiguousarray(ins["L0_attn_W_in"].T),
        "grep": np.ascontiguousarray(np.broadcast_to(ins["L0_attn_ln_in_g"], (128, H))),
        "brep": np.ascontiguousarray(np.broadcast_to(ins["L0_attn_ln_in_b"], (128, H))),
        "identf": np.eye(128, dtype=np.float32),
    }
    maps = []
    for c in range(NCORES):
        b, hh = c // 2, c % 2
        m = dict(common)
        m["xin"] = np.ascontiguousarray(x1_full[b, hh * TLOC:(hh + 1) * TLOC, :])
        maps.append(m)
    res = run_bass_kernel_spmd(nc, maps, core_ids=list(range(NCORES)))
    xz = np.empty((BATCH, I2, SEQ), np.float32)
    for c in range(NCORES):
        b, hh = c // 2, c % 2
        xz[b, :, hh * TLOC:(hh + 1) * TLOC] = res.results[c]["xz"]

    iloc = I // 2
    nc = cached('l0b', build_l0attn_b, iloc, SEQ)
    maps = []
    for c in range(NCORES):
        b, ih = c // 2, c % 2
        sl = slice(ih * iloc, (ih + 1) * iloc)
        maps.append({
            "xs": np.ascontiguousarray(xz[b, sl.start:sl.stop, :]),
            "gate": np.ascontiguousarray(xz[b, I + sl.start:I + sl.stop, :]),
            "alog": np.ascontiguousarray(ins["L0_attn_A_log"][sl]),
            "bmat": np.ascontiguousarray(ins["L0_attn_B"][sl]),
            "cmat": np.ascontiguousarray(ins["L0_attn_C"][sl]),
        })
    res = run_bass_kernel_spmd(nc, maps, core_ids=list(range(NCORES)))
    sy = np.empty((BATCH, I, SEQ), np.float32)
    for c in range(NCORES):
        b, ih = c // 2, c % 2
        sy[b, ih * iloc:(ih + 1) * iloc, :] = res.results[c]["sy"]

    nc = cached('build_l0attn_c', build_l0attn_c, TLOC)
    common = {
        "WoutT": np.ascontiguousarray(ins["L0_attn_W_out"].T),
        "grep": np.ascontiguousarray(np.broadcast_to(ins["L0_attn_ln_out_g"], (128, H))),
        "brep": np.ascontiguousarray(np.broadcast_to(ins["L0_attn_ln_out_b"], (128, H))),
        "identf": np.eye(128, dtype=np.float32),
    }
    maps = []
    for c in range(NCORES):
        b, hh = c // 2, c % 2
        m = dict(common)
        m["syin"] = np.ascontiguousarray(sy[b, :, hh * TLOC:(hh + 1) * TLOC])
        m["x1in"] = np.ascontiguousarray(x1_full[b, hh * TLOC:(hh + 1) * TLOC, :])
        maps.append(m)
    res = run_bass_kernel_spmd(nc, maps, core_ids=list(range(NCORES)))
    x2 = np.empty((BATCH, SEQ, DM), np.float32)
    for c in range(NCORES):
        b, hh = c // 2, c % 2
        x2[b, hh * TLOC:(hh + 1) * TLOC, :] = res.results[c]["x2out"]
    return x2, xz, sy


QTHR = [0.25, 0.75, 1.25, 1.75, 2.5, 2.5, 3.5, 3.5, 5.0, 5.0, 5.0, 5.0]


def emit_quant(cph, src, dst_bf, v, acc_bf, sgn_bf, scl, sclh, W):
    nb = W // 16

    def r3(ap, W=W):
        return ap.rearrange("p (g w) -> p g w", w=16)

    cph("vector", lambda eng: eng.reduce_max(scl[:, 0:nb], r3(src), AX.X,
                                             apply_absolute_value=True))
    cph("vector", lambda eng: eng.tensor_scalar(scl[:, 0:nb], scl[:, 0:nb],
                                                1.0 / 6.0, 1e-12, ALU.mult, ALU.max))
    cph("vector", lambda eng: eng.reciprocal(sclh[:, 0:nb], scl[:, 0:nb]))
    cph("scalar", lambda eng: eng.activation(v[:, 0:W], src, AF.Abs))
    cph("vector", lambda eng: eng.tensor_tensor(
        r3(v[:, 0:W]), r3(v[:, 0:W]),
        sclh[:, 0:nb].unsqueeze(2).broadcast_to([128, nb, 16]), ALU.mult))
    cph("vector", lambda eng: eng.tensor_scalar(acc_bf[:, 0:W], v[:, 0:W],
                                                QTHR[0], None, ALU.is_gt))
    for thr in QTHR[1:]:
        cph("vector", lambda eng, thr=thr: eng.scalar_tensor_tensor(
            acc_bf[:, 0:W], v[:, 0:W], thr, acc_bf[:, 0:W], ALU.is_gt, ALU.add))
    cph("scalar", lambda eng: eng.activation(sgn_bf[:, 0:W], src, AF.Sign))
    cph("vector", lambda eng: eng.tensor_tensor(acc_bf[:, 0:W], acc_bf[:, 0:W],
                                                sgn_bf[:, 0:W], ALU.mult))
    cph("vector", lambda eng: eng.tensor_scalar(sclh[:, 0:nb], scl[:, 0:nb],
                                                0.5, None, ALU.mult))
    cph("vector", lambda eng: eng.tensor_tensor(
        r3(dst_bf[:, 0:W]), r3(acc_bf[:, 0:W]),
        sclh[:, 0:nb].unsqueeze(2).broadcast_to([128, nb, 16]), ALU.mult))


def build_l1ffn_a(ntok):
    nc = bass.Bass("TRN2", target_bir_lowering=False)
    TG = 256
    NG = ntok // TG
    NTT = TG // 128
    H = 1024
    NH = H // 128

    xin = nc.dram_tensor("xin", [ntok, H], F32, kind="ExternalInput").ap()
    identb = nc.dram_tensor("identb", [128, 128], BF16, kind="ExternalInput").ap()
    xqT = nc.dram_tensor("xqT", [H, ntok], BF16, kind="ExternalOutput").ap()

    with contextlib.ExitStack() as st:
        def sb(name, shape, dt=F32):
            return st.enter_context(nc.sbuf_tensor(name, shape, dt))
        sidb = sb("sidb", [128, 128], BF16)
        seps = sb("seps", [128, 1], F32)
        sx = sb("sx", [128, H * NTT], F32)
        shid = sb("shid", [128, H * NTT], F32)
        sv = sb("sv", [128, H * NTT], F32)
        sacc = sb("sacc", [128, H * NTT], BF16)
        ssgn = sb("ssgn", [128, H * NTT], BF16)
        sscl = sb("sscl", [128, H * NTT // 16], F32)
        sscl2 = sb("sscl2", [128, H * NTT // 16], F32)
        smu = sb("smu", [128, NTT], F32)
        ssq = sb("ssq", [128, NTT], F32)
        srstd = sb("srstd", [128, NTT], F32)
        sscr = sb("sscr", [128, H], F32)
        sxqT = sb("sxqT", [128, NH * TG], BF16)
        ptb = st.enter_context(nc.psum_tensor("ptb", [128, 512], BF16))

        chain = st.enter_context(nc.semaphore("chain"))
        block = st.enter_context(nc.Block())
        sch = Sched()
        dph, gph, cph = mkphases(sch, chain)

        dph([(sidb[:], identb)])
        cph("vector", lambda eng: eng.memset(seps[:], 1e-5))

        for g in range(NG):
            t0 = g * TG
            dph([(sx[:, j * H:(j + 1) * H], xin[t0 + 128 * j:t0 + 128 * (j + 1), :])
                 for j in range(NTT)])
            emit_ln(dph, cph, sx, shid, smu, ssq, srstd, sscr, NTT, H, seps[:, 0:1])
            emit_quant(cph, shid[:, 0:H * NTT], sacc, sv, sacc, ssgn, sscl, sscl2, H * NTT)

            for hc in range(NH):
                def f_tp(eng, hc=hc):
                    for j in range(NTT):
                        last = eng.transpose(ptb[:, 128 * j:128 * (j + 1)],
                                             sacc[:, j * H + 128 * hc:j * H + 128 * (hc + 1)],
                                             sidb[:])
                    return last
                cph("tensor", f_tp)
                cph("scalar", lambda eng, hc=hc: eng.copy(
                    sxqT[:, hc * TG:(hc + 1) * TG], ptb[:, 0:TG]))
            dph([(xqT[128 * hc:128 * (hc + 1), t0:t0 + TG],
                  sxqT[:, hc * TG:(hc + 1) * TG]) for hc in range(NH)])

        sch.emit(nc, block, chain)
    return nc


def build_l1ffn_b(ntok):
    nc = bass.Bass("TRN2", target_bir_lowering=False)
    TG = 256
    NG = ntok // TG
    NTT = TG // 128
    H, I = 1024, 2816
    NH, NI = H // 128, I // 128

    xqTin = nc.dram_tensor("xqTin", [H, ntok], BF16, kind="ExternalInput").ap()
    wg = nc.dram_tensor("wg", [I, H], F32, kind="ExternalInput").ap()
    wu = nc.dram_tensor("wu", [I, H], F32, kind="ExternalInput").ap()
    identb = nc.dram_tensor("identb", [128, 128], BF16, kind="ExternalInput").ap()
    identf = nc.dram_tensor("identf", [128, 128], F32, kind="ExternalInput").ap()
    hq_out = nc.dram_tensor("hq", [ntok, I], BF16, kind="ExternalOutput").ap()

    with contextlib.ExitStack() as st:
        def sb(name, shape, dt=F32):
            return st.enter_context(nc.sbuf_tensor(name, shape, dt))
        sidb = sb("sidb", [128, 128], BF16)
        sidf = sb("sidf", [128, 128], F32)
        wgq = sb("wgq", [128, NH * I], BF16)
        wuq = sb("wuq", [128, NH * I], BF16)
        stagw = sb("stagw", [128, 2 * H], F32)
        swq = sb("swq", [128, 2 * H], BF16)
        sv = sb("sv", [128, NTT * I], F32)
        sacc = sb("sacc", [128, NTT * I], BF16)
        ssgn = sb("ssgn", [128, NTT * I], BF16)
        sscl = sb("sscl", [128, NTT * I // 16], F32)
        sscl2 = sb("sscl2", [128, NTT * I // 16], F32)
        sxq = sb("sxq", [128, NH * TG], BF16)
        sup = sb("sup", [128, NI * TG], F32)
        shtok = sb("shtok", [128, NTT * I], F32)

        ptb = st.enter_context(nc.psum_tensor("ptb", [128, 512], BF16))
        pmm = [st.enter_context(nc.psum_tensor(f"pmm{q}", [128, TG], F32)) for q in range(4)]

        chain = st.enter_context(nc.semaphore("chain"))
        block = st.enter_context(nc.Block())
        sch = Sched()
        dph, gph, cph = mkphases(sch, chain)

        dph([(sidb[:], identb), (sidf[:], identf)])

        for (wsrc, wdst) in ((wg, wgq), (wu, wuq)):
            for nt0 in range(0, NI, 2):
                nrow = min(2, NI - nt0)
                W = nrow * H
                dph([(stagw[:, r * H:(r + 1) * H],
                      wsrc[128 * (nt0 + r):128 * (nt0 + r + 1), :]) for r in range(nrow)])
                emit_quant(cph, stagw[:, 0:W], swq, sv, sacc, ssgn, sscl, sscl2, W)
                for r in range(nrow):
                    for half in range(2):
                        def f_tp2(eng, r=r, nt0=nt0, half=half):
                            for kk in range(4):
                                k = half * 4 + kk
                                last = eng.transpose(ptb[:, 128 * kk:128 * (kk + 1)],
                                                     swq[:, r * H + 128 * k:r * H + 128 * (k + 1)],
                                                     sidb[:])
                            return last
                        cph("tensor", f_tp2)

                        def f_ev(eng, r=r, nt0=nt0, half=half, wdst=wdst):
                            for kk in range(4):
                                k = half * 4 + kk
                                last = eng.copy(
                                    wdst[:, k * I + 128 * (nt0 + r):k * I + 128 * (nt0 + r + 1)],
                                    ptb[:, 128 * kk:128 * (kk + 1)])
                            return last
                        cph("scalar", f_ev)

        for g in range(NG):
            t0 = g * TG
            dph([(sxq[:, k * TG:(k + 1) * TG], xqTin[128 * k:128 * (k + 1), t0:t0 + TG])
                 for k in range(NH)])

            for nc0 in range(0, NI, 4):
                nn = min(4, NI - nc0)

                def f_mmu(eng, nc0=nc0, nn=nn):
                    for q in range(nn):
                        mc = nc0 + q
                        for k in range(NH):
                            last = eng.matmul(pmm[q][:],
                                              wuq[:, k * I + 128 * mc:k * I + 128 * (mc + 1)],
                                              sxq[:, k * TG:(k + 1) * TG],
                                              start=(k == 0), stop=(k == NH - 1))
                    return last
                cph("tensor", f_mmu)

                def f_evu(eng, nc0=nc0, nn=nn):
                    for q in range(nn):
                        mc = nc0 + q
                        last = eng.copy(sup[:, mc * TG:(mc + 1) * TG], pmm[q][:])
                    return last
                cph("scalar", f_evu)

            for nc0 in range(0, NI, 4):
                nn = min(4, NI - nc0)

                def f_mmg(eng, nc0=nc0, nn=nn):
                    for q in range(nn):
                        mc = nc0 + q
                        for k in range(NH):
                            last = eng.matmul(pmm[q][:],
                                              wgq[:, k * I + 128 * mc:k * I + 128 * (mc + 1)],
                                              sxq[:, k * TG:(k + 1) * TG],
                                              start=(k == 0), stop=(k == NH - 1))
                    return last
                cph("tensor", f_mmg)

                def f_evg(eng, nc0=nc0, nn=nn):
                    for q in range(nn):
                        mc = nc0 + q
                        last = eng.activation(sv[:, mc * TG:(mc + 1) * TG],
                                              pmm[q][:], AF.Silu)
                    return last
                cph("scalar", f_evg)

                def f_h(eng, nc0=nc0, nn=nn):
                    for q in range(nn):
                        mc = nc0 + q
                        last = eng.tensor_tensor(sup[:, mc * TG:(mc + 1) * TG],
                                                 sup[:, mc * TG:(mc + 1) * TG],
                                                 sv[:, mc * TG:(mc + 1) * TG], ALU.mult)
                    return last
                cph("vector", f_h)

            for j in range(NTT):
                for c0 in range(0, NI, 4):
                    nn = min(4, NI - c0)

                    def f_tp(eng, j=j, c0=c0, nn=nn):
                        for kk in range(nn):
                            ic = c0 + kk
                            last = eng.transpose(pmm[kk][:, 0:128],
                                                 sup[:, ic * TG + 128 * j:ic * TG + 128 * (j + 1)],
                                                 sidf[:])
                        return last
                    cph("tensor", f_tp)

                    def f_ev(eng, j=j, c0=c0, nn=nn):
                        for kk in range(nn):
                            ic = c0 + kk
                            last = eng.copy(shtok[:, j * I + 128 * ic:j * I + 128 * (ic + 1)],
                                            pmm[kk][:, 0:128])
                        return last
                    cph("scalar", f_ev)

            emit_quant(cph, shtok[:, 0:NTT * I], sacc, sv, sacc, ssgn, sscl, sscl2, NTT * I)
            dph([(hq_out[t0 + 128 * j:t0 + 128 * (j + 1), :], sacc[:, j * I:(j + 1) * I])
                 for j in range(NTT)])

        sch.emit(nc, block, chain)
    return nc


def build_l1ffn_d(ntok):
    nc = bass.Bass("TRN2", target_bir_lowering=False)
    TG = 256
    NG = ntok // TG
    NTT = TG // 128
    H, I = 1024, 2816
    NH, NI = H // 128, I // 128

    hqT = nc.dram_tensor("hqT", [I, ntok], BF16, kind="ExternalInput").ap()
    wd = nc.dram_tensor("wd", [H, I], F32, kind="ExternalInput").ap()
    xres = nc.dram_tensor("xres", [ntok, H], F32, kind="ExternalInput").ap()
    identf = nc.dram_tensor("identf", [128, 128], F32, kind="ExternalInput").ap()
    identb = nc.dram_tensor("identb", [128, 128], BF16, kind="ExternalInput").ap()
    xout = nc.dram_tensor("xout", [ntok, H], F32, kind="ExternalOutput").ap()

    with contextlib.ExitStack() as st:
        def sb(name, shape, dt=F32):
            return st.enter_context(nc.sbuf_tensor(name, shape, dt))
        sidf = sb("sidf", [128, 128], F32)
        sidb = sb("sidb", [128, 128], BF16)
        wdq = sb("wdq", [128, NI * H], BF16)
        stagw = sb("stagw", [128, 2 * I], F32)
        swq = sb("swq", [128, 2 * I], BF16)
        sv = sb("sv", [128, 2 * I], F32)
        sacc = sb("sacc", [128, 2 * I], BF16)
        ssgn = sb("ssgn", [128, 2 * I], BF16)
        sscl = sb("sscl", [128, 2 * I // 16], F32)
        sscl2 = sb("sscl2", [128, 2 * I // 16], F32)
        shq = sb("shq", [128, NI * TG], BF16)
        sot = sb("sot", [128, NH * TG], F32)
        sx = sb("sx", [128, NTT * H], F32)

        ptb = st.enter_context(nc.psum_tensor("ptb", [128, 512], BF16))
        pt = st.enter_context(nc.psum_tensor("pt", [128, 1024], F32))
        pmm = [st.enter_context(nc.psum_tensor(f"pmm{q}", [128, TG], F32)) for q in range(4)]

        chain = st.enter_context(nc.semaphore("chain"))
        block = st.enter_context(nc.Block())
        sch = Sched()
        dph, gph, cph = mkphases(sch, chain)

        dph([(sidf[:], identf), (sidb[:], identb)])

        for nt0 in range(0, NH, 2):
            W = 2 * I
            dph([(stagw[:, r * I:(r + 1) * I], wd[128 * (nt0 + r):128 * (nt0 + r + 1), :])
                 for r in range(2)])
            emit_quant(cph, stagw[:, 0:W], swq, sv, sacc, ssgn, sscl, sscl2, W)
            for r in range(2):
                for c0 in range(0, NI, 4):
                    nn = min(4, NI - c0)

                    def f_tp(eng, r=r, c0=c0, nn=nn):
                        for kk in range(nn):
                            k = c0 + kk
                            last = eng.transpose(ptb[:, 128 * kk:128 * (kk + 1)],
                                                 swq[:, r * I + 128 * k:r * I + 128 * (k + 1)],
                                                 sidb[:])
                        return last
                    cph("tensor", f_tp)

                    def f_ev(eng, r=r, c0=c0, nn=nn, nt0=nt0):
                        for kk in range(nn):
                            k = c0 + kk
                            last = eng.copy(
                                wdq[:, k * H + 128 * (nt0 + r):k * H + 128 * (nt0 + r + 1)],
                                ptb[:, 128 * kk:128 * (kk + 1)])
                        return last
                    cph("scalar", f_ev)

        for g in range(NG):
            t0 = g * TG
            dph([(shq[:, k * TG:(k + 1) * TG], hqT[128 * k:128 * (k + 1), t0:t0 + TG])
                 for k in range(NI)])
            for mc0 in range(0, NH, 4):
                def f_mm(eng, mc0=mc0):
                    for q in range(4):
                        mc = mc0 + q
                        for k in range(NI):
                            last = eng.matmul(pmm[q][:],
                                              wdq[:, k * H + 128 * mc:k * H + 128 * (mc + 1)],
                                              shq[:, k * TG:(k + 1) * TG],
                                              start=(k == 0), stop=(k == NI - 1))
                    return last
                cph("tensor", f_mm)

                def f_ev(eng, mc0=mc0):
                    for q in range(4):
                        mc = mc0 + q
                        last = eng.copy(sot[:, mc * TG:(mc + 1) * TG], pmm[q][:])
                    return last
                cph("scalar", f_ev)

            dph([(sx[:, j * H:(j + 1) * H], xres[t0 + 128 * j:t0 + 128 * (j + 1), :])
                 for j in range(NTT)])
            for j in range(NTT):
                def f_tp(eng, j=j):
                    for hc in range(NH):
                        last = eng.transpose(pt[:, 128 * hc:128 * (hc + 1)],
                                             sot[:, hc * TG + 128 * j:hc * TG + 128 * (j + 1)],
                                             sidf[:])
                    return last
                cph("tensor", f_tp)
                cph("vector", lambda eng, j=j: eng.scalar_tensor_tensor(
                    sx[:, j * H:(j + 1) * H], pt[:, 0:H], 0.5,
                    sx[:, j * H:(j + 1) * H], ALU.mult, ALU.add))
            dph([(xout[t0 + 128 * j:t0 + 128 * (j + 1), :], sx[:, j * H:(j + 1) * H])
                 for j in range(NTT)])

        sch.emit(nc, block, chain)
    return nc


def run_l1ffn(x_full, ins):
    H, I = 1024, 2816
    identb = np.eye(128).astype(BF_NP)
    identf = np.eye(128, dtype=np.float32)

    nc = cached('build_l1ffn_a', build_l1ffn_a, TLOC)
    maps = []
    for c in range(NCORES):
        b, hh = c // 2, c % 2
        maps.append({"xin": np.ascontiguousarray(x_full[b, hh * TLOC:(hh + 1) * TLOC, :]),
                     "identb": identb})
    res = run_bass_kernel_spmd(nc, maps, core_ids=list(range(NCORES)))
    xqT = [res.results[c]["xqT"] for c in range(NCORES)]

    nc = cached('build_l1ffn_b', build_l1ffn_b, TLOC)
    maps = []
    for c in range(NCORES):
        maps.append({"xqTin": xqT[c], "wg": ins["L1_ffn_w_gate"],
                     "wu": ins["L1_ffn_w_up"], "identb": identb, "identf": identf})
    res = run_bass_kernel_spmd(nc, maps, core_ids=list(range(NCORES)))
    hq = [res.results[c]["hq"] for c in range(NCORES)]

    nc = cached('build_l1ffn_d', build_l1ffn_d, TLOC)
    maps = []
    for c in range(NCORES):
        b, hh = c // 2, c % 2
        maps.append({"hqT": np.ascontiguousarray(hq[c].T), "wd": ins["L1_ffn_w_down"],
                     "xres": np.ascontiguousarray(x_full[b, hh * TLOC:(hh + 1) * TLOC, :]),
                     "identf": identf, "identb": identb})
    res = run_bass_kernel_spmd(nc, maps, core_ids=list(range(NCORES)))
    out = np.empty((BATCH, SEQ, DM), np.float32)
    for c in range(NCORES):
        b, hh = c // 2, c % 2
        out[b, hh * TLOC:(hh + 1) * TLOC, :] = res.results[c]["xout"]
    return out


def build_l1attn_a(ntok):
    nc = bass.Bass("TRN2", target_bir_lowering=False)
    TG = 256
    NG = ntok // TG
    NTT = TG // 128
    H, J = 1024, 4096
    NH, NJ = H // 128, J // 128

    xin = nc.dram_tensor("xin", [ntok, H], F32, kind="ExternalInput").ap()
    WinT = nc.dram_tensor("WinT", [H, J], F32, kind="ExternalInput").ap()
    grep = nc.dram_tensor("grep", [128, H], F32, kind="ExternalInput").ap()
    identf = nc.dram_tensor("identf", [128, 128], F32, kind="ExternalInput").ap()
    ln4out = nc.dram_tensor("ln4out", [ntok, H], F32, kind="ExternalOutput").ap()
    xz = nc.dram_tensor("xz", [J, ntok], F32, kind="ExternalOutput").ap()

    with contextlib.ExitStack() as st:
        def sb(name, shape, dt=F32):
            return st.enter_context(nc.sbuf_tensor(name, shape, dt))
        winbf = sb("winbf", [128, NH * J], BF16)
        stag = sb("stag", [128, J], F32)
        sgrep = sb("sgrep", [128, H], F32)
        sident = sb("sident", [128, 128], F32)
        seps = sb("seps", [128, 1], F32)
        seps6 = sb("seps6", [128, 1], F32)
        sx = sb("sx", [128, H * NTT], F32)
        shid = sb("shid", [128, H * NTT], F32)
        smu = sb("smu", [128, NTT], F32)
        ssq = sb("ssq", [128, NTT], F32)
        srstd = sb("srstd", [128, NTT], F32)
        sscr = sb("sscr", [128, H], F32)
        shTb = sb("shTb", [128, NH * TG], BF16)
        sxz = sb("sxz", [128, 4 * TG], F32)

        pt = st.enter_context(nc.psum_tensor("pt", [128, 512], F32))
        pmm = [st.enter_context(nc.psum_tensor(f"pmm{q}", [128, TG], F32)) for q in range(4)]

        chain = st.enter_context(nc.semaphore("chain"))
        block = st.enter_context(nc.Block())
        sch = Sched()
        dph, gph, cph = mkphases(sch, chain)

        dph([(sgrep[:], grep), (sident[:], identf)])
        cph("vector", lambda eng: eng.memset(seps[:], 1e-5))
        cph("vector", lambda eng: eng.memset(seps6[:], 1e-6))
        for k in range(NH):
            dph([(stag[:], WinT[128 * k:128 * (k + 1), :])])
            cph("vector", lambda eng, k=k: eng.tensor_copy(winbf[:, k * J:(k + 1) * J], stag[:]))

        for g in range(NG):
            t0 = g * TG
            dph([(sx[:, j * H:(j + 1) * H], xin[t0 + 128 * j:t0 + 128 * (j + 1), :])
                 for j in range(NTT)])
            emit_ln(dph, cph, sx, shid, smu, ssq, srstd, sscr, NTT, H, seps[:, 0:1])
            dph([(ln4out[t0 + 128 * j:t0 + 128 * (j + 1), :], shid[:, j * H:(j + 1) * H])
                 for j in range(NTT)])

            def f_sq(eng):
                for j in range(NTT):
                    last = eng.activation(sscr[:, 0:H], shid[:, j * H:(j + 1) * H],
                                          AF.Square, accum_out=ssq[:, j:j + 1])
                return last
            cph("scalar", f_sq)
            cph("vector", lambda eng: eng.tensor_scalar(ssq[:, 0:NTT], ssq[:, 0:NTT],
                                                        1.0 / H, None, ALU.mult))
            cph("scalar", lambda eng: eng.activation(ssq[:, 0:NTT], ssq[:, 0:NTT],
                                                     AF.Sqrt, bias=seps6[0:128, 0:1]))
            cph("vector", lambda eng: eng.reciprocal(ssq[:, 0:NTT], ssq[:, 0:NTT]))

            def f_n(eng):
                for j in range(NTT):
                    last = eng.tensor_scalar(shid[:, j * H:(j + 1) * H],
                                             shid[:, j * H:(j + 1) * H],
                                             ssq[:, j:j + 1], None, ALU.mult)
                return last
            cph("vector", f_n)

            def f_g(eng):
                for j in range(NTT):
                    last = eng.tensor_tensor(shid[:, j * H:(j + 1) * H],
                                             shid[:, j * H:(j + 1) * H], sgrep[:], ALU.mult)
                return last
            cph("vector", f_g)

            for hc in range(NH):
                def f_tp(eng, hc=hc):
                    for j in range(NTT):
                        last = eng.transpose(pt[:, 128 * j:128 * (j + 1)],
                                             shid[:, j * H + 128 * hc:j * H + 128 * (hc + 1)],
                                             sident[:])
                    return last
                cph("tensor", f_tp)
                cph("scalar", lambda eng, hc=hc: eng.copy(
                    shTb[:, hc * TG:(hc + 1) * TG], pt[:, 0:TG]))

            for jc0 in range(0, NJ, 4):
                def f_mm(eng, jc0=jc0):
                    for q in range(4):
                        jc = jc0 + q
                        for k in range(NH):
                            last = eng.matmul(pmm[q][:],
                                              winbf[:, k * J + 128 * jc:k * J + 128 * (jc + 1)],
                                              shTb[:, k * TG:(k + 1) * TG],
                                              start=(k == 0), stop=(k == NH - 1))
                    return last
                cph("tensor", f_mm)

                def f_ev(eng, jc0=jc0):
                    for q in range(4):
                        last = eng.copy(sxz[:, q * TG:(q + 1) * TG], pmm[q][:])
                    return last
                cph("scalar", f_ev)
                dph([(xz[128 * (jc0 + q):128 * (jc0 + q + 1), t0:t0 + TG],
                      sxz[:, q * TG:(q + 1) * TG]) for q in range(4)])

        sch.emit(nc, block, chain)
    return nc


def build_l1attn_b2(ntok, first_half):
    nc = bass.Bass("TRN2", target_bir_lowering=False)
    TGc = 512
    NG = ntok // TGc
    I, P = 2048, 33
    NIc = I // 128
    W = TGc + 3

    xs = nc.dram_tensor("xs", [I, ntok], F32, kind="ExternalInput").ap()
    xsprev = nc.dram_tensor("xsprev", [I, 4], F32, kind="ExternalInput").ap()
    convw = nc.dram_tensor("convw", [I, 4], F32, kind="ExternalInput").ap()
    convb = nc.dram_tensor("convb", [I, 1], F32, kind="ExternalInput").ap()
    WxT = nc.dram_tensor("WxT", [I, P], F32, kind="ExternalInput").ap()
    xc_out = nc.dram_tensor("xc", [I, ntok], F32, kind="ExternalOutput").ap()
    prm_out = nc.dram_tensor("prm", [P, ntok], F32, kind="ExternalOutput").ap()

    with contextlib.ExitStack() as st:
        def sb(name, shape, dt=F32):
            return st.enter_context(nc.sbuf_tensor(name, shape, dt))
        scw = sb("scw", [128, NIc * 4], F32)
        scb = sb("scb", [128, NIc], F32)
        swx = sb("swx", [128, NIc * P], F32)
        swxb = sb("swxb", [128, NIc * P], BF16)
        sxs = sb("sxs", [128, NIc * W], F32)
        sxc = sb("sxc", [128, NIc * TGc], F32)
        sxcb = sb("sxcb", [128, NIc * TGc], BF16)
        sprm = sb("sprm", [33, TGc], F32)

        pp = st.enter_context(nc.psum_tensor("pp", [33, TGc], F32))

        chain = st.enter_context(nc.semaphore("chain"))
        block = st.enter_context(nc.Block())
        sch = Sched()
        dph, gph, cph = mkphases(sch, chain)

        dph([(scw[:, it * 4:(it + 1) * 4], convw[128 * it:128 * (it + 1), :])
             for it in range(NIc)])
        dph([(scb[:, it:it + 1], convb[128 * it:128 * (it + 1), :])
             for it in range(NIc)])
        dph([(swx[:, it * P:(it + 1) * P], WxT[128 * it:128 * (it + 1), :])
             for it in range(NIc)])
        cph("vector", lambda eng: eng.tensor_copy(swxb[:], swx[:]))

        for g in range(NG):
            t0 = g * TGc
            if g == 0:
                if first_half:
                    cph("vector", lambda eng: eng.memset(sxs[:], 0.0))
                else:
                    dph([(sxs[:, it * W:it * W + 3], xsprev[128 * it:128 * (it + 1), 1:4])
                         for it in range(NIc)])
                dph([(sxs[:, it * W + 3:(it + 1) * W],
                      xs[128 * it:128 * (it + 1), t0:t0 + TGc]) for it in range(NIc)])
            else:
                dph([(sxs[:, it * W:(it + 1) * W],
                      xs[128 * it:128 * (it + 1), t0 - 3:t0 + TGc]) for it in range(NIc)])

            def f_c0(eng):
                for it in range(NIc):
                    last = eng.tensor_scalar(sxc[:, it * TGc:(it + 1) * TGc],
                                             sxs[:, it * W:it * W + TGc],
                                             scw[:, it * 4:it * 4 + 1], None, ALU.mult)
                return last
            cph("vector", f_c0)
            for tau in range(1, 4):
                def f_ct(eng, tau=tau):
                    for it in range(NIc):
                        last = eng.scalar_tensor_tensor(
                            sxc[:, it * TGc:(it + 1) * TGc],
                            sxs[:, it * W + tau:it * W + tau + TGc],
                            scw[:, it * 4 + tau:it * 4 + tau + 1],
                            sxc[:, it * TGc:(it + 1) * TGc], ALU.mult, ALU.add)
                    return last
                cph("vector", f_ct)

            def f_cb(eng):
                for it in range(NIc):
                    last = eng.tensor_scalar(sxc[:, it * TGc:(it + 1) * TGc],
                                             sxc[:, it * TGc:(it + 1) * TGc],
                                             scb[:, it:it + 1], None, ALU.add)
                return last
            cph("vector", f_cb)

            def f_silu(eng):
                for it in range(NIc):
                    last = eng.activation(sxc[:, it * TGc:(it + 1) * TGc],
                                          sxc[:, it * TGc:(it + 1) * TGc], AF.Silu)
                return last
            cph("scalar", f_silu)
            dph([(xc_out[128 * it:128 * (it + 1), t0:t0 + TGc],
                  sxc[:, it * TGc:(it + 1) * TGc]) for it in range(NIc)])
            cph("vector", lambda eng: eng.tensor_copy(sxcb[:], sxc[:]))

            def f_mm(eng):
                for k in range(NIc):
                    last = eng.matmul(pp[:], swxb[:, k * P:(k + 1) * P],
                                      sxcb[:, k * TGc:(k + 1) * TGc],
                                      start=(k == 0), stop=(k == NIc - 1))
                return last
            cph("tensor", f_mm)
            cph("scalar", lambda eng: eng.copy(sprm[:], pp[:]))
            dph([(prm_out[:, t0:t0 + TGc], sprm[:])])

        sch.emit(nc, block, chain)
    return nc


def build_l1attn_c(iloc, L):
    nc = bass.Bass("TRN2", target_bir_lowering=False)
    NI = iloc // 128
    D = 16
    TC = 512
    NTC = L // TC

    xc = nc.dram_tensor("xcin", [iloc, L], F32, kind="ExternalInput").ap()
    zin = nc.dram_tensor("zin", [iloc, L], F32, kind="ExternalInput").ap()
    prm = nc.dram_tensor("prm", [33, L], F32, kind="ExternalInput").ap()
    wdt = nc.dram_tensor("wdt", [iloc, 1], F32, kind="ExternalInput").ap()
    bdt = nc.dram_tensor("bdt", [iloc, 1], F32, kind="ExternalInput").ap()
    alog = nc.dram_tensor("alog", [iloc, D], F32, kind="ExternalInput").ap()
    dvec = nc.dram_tensor("dvec", [iloc, 1], F32, kind="ExternalInput").ap()
    onescol128 = nc.dram_tensor("onescol128", [1, 128], F32, kind="ExternalInput").ap()
    sy_out = nc.dram_tensor("syout", [iloc, L], F32, kind="ExternalOutput").ap()

    with contextlib.ExitStack() as st:
        def sb(name, shape, dt=F32):
            return st.enter_context(nc.sbuf_tensor(name, shape, dt))
        swdt = sb("swdt", [128, NI], F32)
        sbdt = sb("sbdt", [128, NI], F32)
        sA = sb("sA", [128, NI * D], F32)
        sD = sb("sD", [128, NI], F32)
        sones1 = sb("sones1", [128, 1], F32)
        sonesc = sb("sonesc", [1, 128], F32)
        sstate = sb("sstate", [128, NI * D], F32)
        sldrow = sb("sldrow", [1, TC], F32)
        sbrow = sb("sbrow", [1, D * TC], F32)
        scrow = sb("scrow", [1, D * TC], F32)
        sldrep = sb("sldrep", [128, TC], F32)
        sbrep = sb("sbrep", [128, D * TC], BF16)
        screp = sb("screp", [128, D * TC], BF16)
        sxc = sb("sxct", [128, NI * TC], F32)
        sdt = sb("sdt", [128, NI * TC], F32)
        sy = sb("syt", [128, NI * TC], F32)
        sa8 = sb("sa8", [128, NI * TC], F32)
        sh = sb("sh", [128, NI * TC], F32)
        swt = sb("swt", [128, NI * TC], BF16)
        su = sb("su", [128, NI * TC], BF16)

        prep = [st.enter_context(nc.psum_tensor(f"prep{q}", [128, TC], F32))
                for q in range(4)]

        chain = st.enter_context(nc.semaphore("chain"))
        block = st.enter_context(nc.Block())
        sch = Sched()
        dph, gph, cph = mkphases(sch, chain)

        dph([(swdt[:, it:it + 1], wdt[128 * it:128 * (it + 1), :]) for it in range(NI)])
        dph([(sbdt[:, it:it + 1], bdt[128 * it:128 * (it + 1), :]) for it in range(NI)])
        dph([(sA[:, it * D:(it + 1) * D], alog[128 * it:128 * (it + 1), :])
             for it in range(NI)])
        dph([(sD[:, it:it + 1], dvec[128 * it:128 * (it + 1), :]) for it in range(NI)])
        dph([(sonesc[:], onescol128)])
        cph("vector", lambda eng: eng.memset(sones1[:], 1.0))
        cph("vector", lambda eng: eng.memset(sstate[:], 0.0))
        cph("scalar", lambda eng: eng.activation(sA[:], sA[:], AF.Exp))
        cph("vector", lambda eng: eng.tensor_scalar(sA[:], sA[:], -1.0, None, ALU.mult))

        for tc in range(NTC):
            c0 = tc * TC
            dph([(sxc[:, it * TC:(it + 1) * TC], xc[128 * it:128 * (it + 1), c0:c0 + TC])
                 for it in range(NI)])
            dph([(sldrow[:], prm[32:33, c0:c0 + TC]),
                 (sbrow[:], prm[0:16, c0:c0 + TC]),
                 (scrow[:], prm[16:32, c0:c0 + TC])])
            cph("tensor", lambda eng: eng.matmul(prep[0][:], sonesc[:], sldrow[:],
                                                 start=True, stop=True))
            cph("scalar", lambda eng: eng.copy(sldrep[:], prep[0][:]))
            for blk, (rows, dstrep) in enumerate(((sbrow, sbrep), (scrow, screp))):
                for d0 in range(0, D, 4):
                    def f_mm(eng, rows=rows, d0=d0):
                        for q in range(4):
                            last = eng.matmul(prep[q][:], sonesc[:],
                                              rows[0:1, (d0 + q) * TC:(d0 + q + 1) * TC],
                                              start=True, stop=True)
                        return last
                    cph("tensor", f_mm)

                    def f_ev(eng, dstrep=dstrep, d0=d0):
                        for q in range(4):
                            last = eng.copy(dstrep[:, (d0 + q) * TC:(d0 + q + 1) * TC],
                                            prep[q][:])
                        return last
                    cph("scalar", f_ev)

            def f_d1(eng):
                for it in range(NI):
                    last = eng.tensor_scalar(sdt[:, it * TC:(it + 1) * TC], sldrep[:],
                                             swdt[:, it:it + 1], None, ALU.mult)
                return last
            cph("vector", f_d1)

            def f_d2(eng):
                for it in range(NI):
                    last = eng.tensor_scalar(sdt[:, it * TC:(it + 1) * TC],
                                             sdt[:, it * TC:(it + 1) * TC],
                                             sbdt[:, it:it + 1], None, ALU.add)
                return last
            cph("vector", f_d2)
            cph("scalar", lambda eng: eng.activation(sdt[:], sdt[:], AF.Exp))
            cph("scalar", lambda eng: eng.activation(sdt[:], sdt[:], AF.Ln,
                                                     bias=sones1[:, 0:1]))
            cph("vector", lambda eng: eng.tensor_tensor(swt[:], sdt[:], sxc[:], ALU.mult))
            cph("vector", lambda eng: eng.memset(sy[:], 0.0))

            for d in range(D):
                def f_a(eng, d=d):
                    for it in range(NI):
                        last = eng.activation(sa8[:, it * TC:(it + 1) * TC],
                                              sdt[:, it * TC:(it + 1) * TC], AF.Exp,
                                              scale=sA[:, it * D + d:it * D + d + 1])
                    return last
                cph("scalar", f_a)

                def f_u(eng, d=d):
                    for it in range(NI):
                        last = eng.tensor_tensor(su[:, it * TC:(it + 1) * TC],
                                                 swt[:, it * TC:(it + 1) * TC],
                                                 sbrep[:, d * TC:(d + 1) * TC], ALU.mult)
                    return last
                cph("vector", f_u)

                def f_scan(eng, d=d):
                    for it in range(NI):
                        last = eng.tensor_tensor_scan(
                            sh[:, it * TC:(it + 1) * TC],
                            sa8[:, it * TC:(it + 1) * TC],
                            su[:, it * TC:(it + 1) * TC],
                            sstate[:, it * D + d:it * D + d + 1],
                            ALU.mult, ALU.add)
                    return last
                cph("vector", f_scan)

                def f_sv(eng, d=d):
                    for it in range(NI):
                        last = eng.tensor_copy(sstate[:, it * D + d:it * D + d + 1],
                                               sh[:, (it + 1) * TC - 1:(it + 1) * TC])
                    return last
                cph("vector", f_sv)

                def f_hc(eng, d=d):
                    for it in range(NI):
                        last = eng.tensor_tensor(su[:, it * TC:(it + 1) * TC],
                                                 sh[:, it * TC:(it + 1) * TC],
                                                 screp[:, d * TC:(d + 1) * TC], ALU.mult)
                    return last
                cph("vector", f_hc)

                def f_acc(eng, d=d):
                    return eng.tensor_tensor(sy[:], sy[:], su[:], ALU.add)
                cph("vector", f_acc)

            def f_xd(eng):
                for it in range(NI):
                    last = eng.tensor_scalar(sh[:, it * TC:(it + 1) * TC],
                                             sxc[:, it * TC:(it + 1) * TC],
                                             sD[:, it:it + 1], None, ALU.mult)
                return last
            cph("vector", f_xd)
            cph("vector", lambda eng: eng.tensor_tensor(sy[:], sy[:], sh[:], ALU.add))
            dph([(sa8[:, it * TC:(it + 1) * TC], zin[128 * it:128 * (it + 1), c0:c0 + TC])
                 for it in range(NI)])
            cph("scalar", lambda eng: eng.activation(sa8[:], sa8[:], AF.Silu))
            cph("vector", lambda eng: eng.tensor_tensor(sy[:], sy[:], sa8[:], ALU.mult))
            dph([(sy_out[128 * it:128 * (it + 1), c0:c0 + TC],
                  sy[:, it * TC:(it + 1) * TC]) for it in range(NI)])

        sch.emit(nc, block, chain)
    return nc


def build_l1attn_d(ntok):
    nc = bass.Bass("TRN2", target_bir_lowering=False)
    TG = 256
    NG = ntok // TG
    NTT = TG // 128
    H, I = 1024, 2048
    NH, NIc = H // 128, I // 128

    syin = nc.dram_tensor("syin", [I, ntok], F32, kind="ExternalInput").ap()
    WoutT = nc.dram_tensor("WoutT", [I, H], F32, kind="ExternalInput").ap()
    x4in = nc.dram_tensor("x4in", [ntok, H], F32, kind="ExternalInput").ap()
    ln4in = nc.dram_tensor("ln4in", [ntok, H], F32, kind="ExternalInput").ap()
    identf = nc.dram_tensor("identf", [128, 128], F32, kind="ExternalInput").ap()
    x5out = nc.dram_tensor("x5out", [ntok, H], F32, kind="ExternalOutput").ap()

    with contextlib.ExitStack() as st:
        def sb(name, shape, dt=F32):
            return st.enter_context(nc.sbuf_tensor(name, shape, dt))
        woutbf = sb("woutbf", [128, NIc * H], BF16)
        stag = sb("stag", [128, H], F32)
        sident = sb("sident", [128, 128], F32)
        ssy = sb("ssy", [128, NIc * TG], F32)
        ssyb = sb("ssyb", [128, NIc * TG], BF16)
        sot = sb("sot", [128, NH * TG], F32)
        sx = sb("sx", [128, H * NTT], F32)
        sl = sb("sl", [128, H * NTT], F32)

        pt = st.enter_context(nc.psum_tensor("pt", [128, 1024], F32))
        pmm = [st.enter_context(nc.psum_tensor(f"pmm{q}", [128, TG], F32)) for q in range(4)]

        chain = st.enter_context(nc.semaphore("chain"))
        block = st.enter_context(nc.Block())
        sch = Sched()
        dph, gph, cph = mkphases(sch, chain)

        dph([(sident[:], identf)])
        for k in range(NIc):
            dph([(stag[:], WoutT[128 * k:128 * (k + 1), :])])
            cph("vector", lambda eng, k=k: eng.tensor_copy(woutbf[:, k * H:(k + 1) * H], stag[:]))

        for g in range(NG):
            t0 = g * TG
            dph([(ssy[:, k * TG:(k + 1) * TG], syin[128 * k:128 * (k + 1), t0:t0 + TG])
                 for k in range(NIc)])
            cph("vector", lambda eng: eng.tensor_copy(ssyb[:], ssy[:]))
            for mc0 in range(0, NH, 4):
                def f_mm(eng, mc0=mc0):
                    for q in range(4):
                        mc = mc0 + q
                        for k in range(NIc):
                            last = eng.matmul(pmm[q][:],
                                              woutbf[:, k * H + 128 * mc:k * H + 128 * (mc + 1)],
                                              ssyb[:, k * TG:(k + 1) * TG],
                                              start=(k == 0), stop=(k == NIc - 1))
                    return last
                cph("tensor", f_mm)

                def f_ev(eng, mc0=mc0):
                    for q in range(4):
                        mc = mc0 + q
                        last = eng.copy(sot[:, mc * TG:(mc + 1) * TG], pmm[q][:])
                    return last
                cph("scalar", f_ev)

            dph([(sx[:, j * H:(j + 1) * H], x4in[t0 + 128 * j:t0 + 128 * (j + 1), :])
                 for j in range(NTT)])
            dph([(sl[:, j * H:(j + 1) * H], ln4in[t0 + 128 * j:t0 + 128 * (j + 1), :])
                 for j in range(NTT)])
            for j in range(NTT):
                def f_tp(eng, j=j):
                    for hc in range(NH):
                        last = eng.transpose(pt[:, 128 * hc:128 * (hc + 1)],
                                             sot[:, hc * TG + 128 * j:hc * TG + 128 * (j + 1)],
                                             sident[:])
                    return last
                cph("tensor", f_tp)
                cph("vector", lambda eng, j=j: eng.tensor_tensor(
                    sx[:, j * H:(j + 1) * H], sx[:, j * H:(j + 1) * H],
                    sl[:, j * H:(j + 1) * H], ALU.add))
                cph("vector", lambda eng, j=j: eng.tensor_tensor(
                    sx[:, j * H:(j + 1) * H], sx[:, j * H:(j + 1) * H],
                    pt[:, 0:H], ALU.add))
            dph([(x5out[t0 + 128 * j:t0 + 128 * (j + 1), :], sx[:, j * H:(j + 1) * H])
                 for j in range(NTT)])

        sch.emit(nc, block, chain)
    return nc


def run_l1attn(x4_full, ins):
    H, J, I = 1024, 4096, 2048
    identf = np.eye(128, dtype=np.float32)

    nc = cached('build_l1attn_a', build_l1attn_a, TLOC)
    common = {
        "WinT": np.ascontiguousarray(ins["L1_attn_W_in"].T),
        "grep": np.ascontiguousarray(np.broadcast_to(ins["L1_attn_g"], (128, H))),
        "identf": identf,
    }
    maps = []
    for c in range(NCORES):
        b, hh = c // 2, c % 2
        m = dict(common)
        m["xin"] = np.ascontiguousarray(x4_full[b, hh * TLOC:(hh + 1) * TLOC, :])
        maps.append(m)
    res = run_bass_kernel_spmd(nc, maps, core_ids=list(range(NCORES)))
    ln4 = np.empty((BATCH, SEQ, H), np.float32)
    xz = np.empty((BATCH, J, SEQ), np.float32)
    for c in range(NCORES):
        b, hh = c // 2, c % 2
        ln4[b, hh * TLOC:(hh + 1) * TLOC, :] = res.results[c]["ln4out"]
        xz[b, :, hh * TLOC:(hh + 1) * TLOC] = res.results[c]["xz"]

    nc0 = cached('l1b2', build_l1attn_b2, TLOC, True)
    nc1 = cached('l1b2', build_l1attn_b2, TLOC, False)
    common = {
        "convw": ins["L1_attn_conv_w"],
        "convb": ins["L1_attn_conv_b"].reshape(I, 1),
        "WxT": np.ascontiguousarray(ins["L1_attn_W_x"].T),
    }
    maps0, maps1 = [], []
    for c in range(NCORES):
        b, hh = c // 2, c % 2
        m = dict(common)
        m["xs"] = np.ascontiguousarray(xz[b, :I, hh * TLOC:(hh + 1) * TLOC])
        m["xsprev"] = np.ascontiguousarray(xz[b, :I, TLOC - 4:TLOC]) if hh == 1 else \
            np.zeros((I, 4), np.float32)
        (maps1 if hh else maps0).append(m)
    res0 = run_bass_kernel_spmd(nc0, maps0, core_ids=[0, 2, 4, 6])
    res1 = run_bass_kernel_spmd(nc1, maps1, core_ids=[0, 2, 4, 6])
    xc = np.empty((BATCH, I, SEQ), np.float32)
    prm = np.empty((BATCH, 33, SEQ), np.float32)
    for b in range(BATCH):
        xc[b, :, 0:TLOC] = res0.results[b]["xc"]
        xc[b, :, TLOC:] = res1.results[b]["xc"]
        prm[b, :, 0:TLOC] = res0.results[b]["prm"]
        prm[b, :, TLOC:] = res1.results[b]["prm"]

    iloc = I // 2
    nc = cached('l1c', build_l1attn_c, iloc, SEQ)
    maps = []
    for c in range(NCORES):
        b, ih = c // 2, c % 2
        sl = slice(ih * iloc, (ih + 1) * iloc)
        maps.append({
            "xcin": np.ascontiguousarray(xc[b, sl, :]),
            "zin": np.ascontiguousarray(xz[b, I + sl.start:I + sl.stop, :]),
            "prm": np.ascontiguousarray(prm[b]),
            "wdt": np.ascontiguousarray(ins["L1_attn_W_dt"][sl]),
            "bdt": np.ascontiguousarray(ins["L1_attn_b_dt"][sl].reshape(iloc, 1)),
            "alog": np.ascontiguousarray(ins["L1_attn_A_log"][sl]),
            "dvec": np.ascontiguousarray(ins["L1_attn_D"][sl].reshape(iloc, 1)),
            "onescol128": np.ones((1, 128), np.float32),
        })
    res = run_bass_kernel_spmd(nc, maps, core_ids=list(range(NCORES)))
    sy = np.empty((BATCH, I, SEQ), np.float32)
    for c in range(NCORES):
        b, ih = c // 2, c % 2
        sy[b, ih * iloc:(ih + 1) * iloc, :] = res.results[c]["syout"]

    nc = cached('build_l1attn_d', build_l1attn_d, TLOC)
    common = {"WoutT": np.ascontiguousarray(ins["L1_attn_W_out"].T), "identf": identf}
    maps = []
    for c in range(NCORES):
        b, hh = c // 2, c % 2
        m = dict(common)
        m["syin"] = np.ascontiguousarray(sy[b, :, hh * TLOC:(hh + 1) * TLOC])
        m["x4in"] = np.ascontiguousarray(x4_full[b, hh * TLOC:(hh + 1) * TLOC, :])
        m["ln4in"] = np.ascontiguousarray(ln4[b, hh * TLOC:(hh + 1) * TLOC, :])
        maps.append(m)
    res = run_bass_kernel_spmd(nc, maps, core_ids=list(range(NCORES)))
    x5 = np.empty((BATCH, SEQ, DM), np.float32)
    for c in range(NCORES):
        b, hh = c // 2, c % 2
        x5[b, hh * TLOC:(hh + 1) * TLOC, :] = res.results[c]["x5out"]
    return x5, xz, xc, prm, sy


def kernel(**inputs):
    x0 = np.ascontiguousarray(inputs["L0_attn_x"], dtype=np.float32)
    x1, _ = run_l0ffn(x0, inputs)
    x2, _, _ = run_l0attn(x1, inputs)
    x3, _ = run_l0ffn(x2, inputs)
    x4 = run_l1ffn(x3, inputs)
    x5, _, _, _, _ = run_l1attn(x4, inputs)
    x6 = run_l1ffn(x5, inputs)
    return x6



# version 02
# speedup vs baseline: 1.0011x; 1.0011x over previous
import contextlib
import numpy as np
import ml_dtypes
import concourse.bass as bass
import concourse.mybir as mybir
from concourse.bass_utils import run_bass_kernel_spmd
from concourse.bass_core_rust import IndirectOffsetOnAxis

F32 = mybir.dt.float32
BF16 = mybir.dt.bfloat16
I32 = mybir.dt.int32
AF = mybir.ActivationFunctionType
ALU = mybir.AluOpType
AX = mybir.AxisListType
BF_NP = ml_dtypes.bfloat16

NCORES = 8
BATCH, SEQ, DM = 4, 8192, 1024
TLOC = SEQ // 2


_PROG_CACHE = {}


def cached(name, fn, *args):
    key = (name,) + args
    if key not in _PROG_CACHE:
        _PROG_CACHE[key] = fn(*args)
    return _PROG_CACHE[key]


class Sched:

    def __init__(self):
        self.phases = []

    def dma(self, fn, n, queue="sync"):
        self.phases.append((queue, fn, 16 * n))

    def comp(self, engine, fn):
        self.phases.append((engine, fn, 1))

    def emit(self, nc, block, chain):
        starts, v = [], 0
        for (_, _, n) in self.phases:
            starts.append(v)
            v += n
        total = v
        phases, ph_starts = self.phases, starts

        def run_engine(name, eng):
            for (e, fn, n), stt in zip(phases, ph_starts):
                if e == name:
                    eng.wait_ge(chain, stt)
                    fn(eng)
            if name == "sync":
                eng.wait_ge(chain, total)

        @block.sync
        def _(eng):
            run_engine("sync", eng)

        @block.vector
        def _(eng):
            run_engine("vector", eng)

        @block.scalar
        def _(eng):
            run_engine("scalar", eng)

        @block.tensor
        def _(eng):
            run_engine("tensor", eng)

        @block.gpsimd
        def _(eng):
            run_engine("gpsimd", eng)
        return total


def mkphases(sch, chain):
    def dph(pairs, queue="sync"):
        pairs = list(pairs)

        def fn(eng):
            for dst, src in pairs:
                eng.dma_start(dst, src).then_inc(chain, 16)
        sch.dma(fn, len(pairs), queue)

    def gph(fn_emit):
        sch.dma(fn_emit, 1, "gpsimd")

    def cph(engine, body):
        def fn(eng):
            body(eng).then_inc(chain, 1)
        sch.comp(engine, fn)
    return dph, gph, cph


def emit_ln(dph, cph, sx, sout, smu, ssq, srstd, sscr, ntiles, H, eps_ap=None):
    def f_sq(eng):
        for j in range(ntiles):
            last = eng.activation(sscr[:, 0:H], sx[:, j * H:(j + 1) * H],
                                  AF.Square, accum_out=ssq[:, j:j + 1])
        return last
    cph("scalar", f_sq)

    def f_su(eng):
        for j in range(ntiles):
            last = eng.reduce_sum(smu[:, j:j + 1], sx[:, j * H:(j + 1) * H], AX.X)
        return last
    cph("vector", f_su)

    def f_sc(eng):
        eng.tensor_scalar(smu[:, 0:ntiles], smu[:, 0:ntiles], 1.0 / H, None, ALU.mult)
        return eng.tensor_scalar(ssq[:, 0:ntiles], ssq[:, 0:ntiles], 1.0 / H, None, ALU.mult)
    cph("vector", f_sc)

    def f_mu2(eng):
        return eng.tensor_tensor(srstd[:, 0:ntiles], smu[:, 0:ntiles], smu[:, 0:ntiles], ALU.mult)
    cph("vector", f_mu2)

    def f_var(eng):
        return eng.tensor_tensor(srstd[:, 0:ntiles], ssq[:, 0:ntiles], srstd[:, 0:ntiles], ALU.subtract)
    cph("vector", f_var)

    def f_sqrt(eng):
        return eng.activation(srstd[:, 0:ntiles], srstd[:, 0:ntiles], AF.Sqrt, bias=eps_ap)
    cph("scalar", f_sqrt)

    def f_rcp(eng):
        return eng.reciprocal(srstd[:, 0:ntiles], srstd[:, 0:ntiles])
    cph("vector", f_rcp)

    def f_sub(eng):
        for j in range(ntiles):
            last = eng.tensor_scalar(sout[:, j * H:(j + 1) * H], sx[:, j * H:(j + 1) * H],
                                     smu[:, j:j + 1], None, ALU.subtract)
        return last
    cph("vector", f_sub)

    def f_mul(eng):
        for j in range(ntiles):
            last = eng.tensor_scalar(sout[:, j * H:(j + 1) * H], sout[:, j * H:(j + 1) * H],
                                     srstd[:, j:j + 1], None, ALU.mult)
        return last
    cph("vector", f_mul)


def build_l0ffn(ntok):
    nc = bass.Bass("TRN2", target_bir_lowering=False)
    TG = 256
    NG = ntok // TG
    NTT = TG // 128
    H, F = 1024, 2048
    NH = H // 128

    xin = nc.dram_tensor("xin", [ntok, H], F32, kind="ExternalInput").ap()
    xlast = nc.dram_tensor("xlast", [8, 128], F32, kind="ExternalInput").ap()
    Wp1 = nc.dram_tensor("Wp1", [128, 8 * 128], F32, kind="ExternalInput").ap()
    bp1 = nc.dram_tensor("bp1", [128, 1], F32, kind="ExternalInput").ap()
    Wp2 = nc.dram_tensor("Wp2", [128, 8], F32, kind="ExternalInput").ap()
    bp2 = nc.dram_tensor("bp2", [8, 1], F32, kind="ExternalInput").ap()
    W1r = nc.dram_tensor("W1r", [8 * H, F], F32, kind="ExternalInput").ap()
    W2r = nc.dram_tensor("W2r", [8 * F, H], F32, kind="ExternalInput").ap()
    b1r = nc.dram_tensor("b1r", [8, F], F32, kind="ExternalInput").ap()
    b2r = nc.dram_tensor("b2r", [8, H], F32, kind="ExternalInput").ap()
    Wo = nc.dram_tensor("Wo", [H, H], F32, kind="ExternalInput").ap()
    bo_row = nc.dram_tensor("bo_row", [1, H], F32, kind="ExternalInput").ap()
    identf = nc.dram_tensor("identf", [128, 128], F32, kind="ExternalInput").ap()
    ones_row = nc.dram_tensor("ones_row", [1, TG], F32, kind="ExternalInput").ap()
    onescol128 = nc.dram_tensor("onescol128", [1, 128], F32, kind="ExternalInput").ap()
    iota128 = nc.dram_tensor("iota128", [128, 1], F32, kind="ExternalInput").ap()
    iota8 = nc.dram_tensor("iota8", [8, 1], F32, kind="ExternalInput").ap()
    revbig = nc.dram_tensor("revbig", [1, 8], F32, kind="ExternalInput").ap()

    xout = nc.dram_tensor("xout", [ntok, H], F32, kind="ExternalOutput").ap()
    o_sel = nc.dram_tensor("o_sel", [1, 1], F32, kind="ExternalOutput").ap()

    with contextlib.ExitStack() as st:
        def sb(name, shape, dt=F32):
            return st.enter_context(nc.sbuf_tensor(name, shape, dt))

        w1bf = sb("w1bf", [128, 8 * F], BF16)
        w2bf = sb("w2bf", [128, 16 * H], BF16)
        wobf = sb("wobf", [128, 8 * H], BF16)
        stag = sb("stag", [128, F], F32)
        b1row = sb("b1row", [1, F], F32)
        b2row = sb("b2row", [1, H], F32)
        borow = sb("borow", [1, H], F32)
        onesr = sb("onesr", [1, TG], F32)
        onesc = sb("onesc", [1, 128], F32)
        siota = sb("siota", [128, 1], F32)
        sident = sb("sident", [128, 128], F32)

        sxl = sb("sxl", [8, 128], F32)
        sxls = sb("sxls", [8, 128], F32)
        sones8 = sb("sones8", [8, 1], F32)
        sones18 = sb("sones18", [1, 8], F32)
        sstat = sb("sstat", [8, 2], F32)
        st11 = sb("st11", [1, 4], F32)
        swp1 = sb("swp1", [128, 8 * 128], F32)
        sbp1 = sb("sbp1", [128, 1], F32)
        swp2 = sb("swp2", [128, 8], F32)
        sbp2 = sb("sbp2", [8, 1], F32)
        sxlT = sb("sxlT", [128, 8], F32)
        sg1 = sb("sg1", [128, 1], F32)
        slogT = sb("slogT", [8, 1], F32)
        slog = sb("slog", [1, 8], F32)
        srb = sb("srb", [1, 8], F32)
        sm1 = sb("sm1", [1, 1], F32)
        sarg = sb("sarg", [1, 8], F32)
        ssel = sb("ssel", [1, 1], F32)
        sidxf = sb("sidxf", [128, 1], F32)
        sidx = sb("sidx", [128, 1], I32)
        sselrep = sb("sselrep", [128, 1], F32)
        seps = sb("seps", [128, 1], F32)
        siota8 = sb("siota8", [8, 1], F32)
        sonehot = sb("sonehot", [8, 1], F32)
        b1all = sb("b1all", [8, F], F32)
        b2all = sb("b2all", [8, H], F32)

        sx = sb("sx", [128, H * NTT], F32)
        shid = sb("shid", [128, H * NTT], F32)
        smu = sb("smu", [128, NTT], F32)
        ssq = sb("ssq", [128, NTT], F32)
        srstd = sb("srstd", [128, NTT], F32)
        sscr = sb("sscr", [128, H], F32)
        shTb = sb("shTb", [128, 8 * TG], BF16)
        sh1 = sb("sh1", [128, 16 * TG], BF16)
        sr1 = sb("sr1", [128, 8 * TG], BF16)
        so1 = sb("so1", [128, 8 * TG], F32)

        pt = st.enter_context(nc.psum_tensor("pt", [128, 1024], F32))
        pmm = [st.enter_context(nc.psum_tensor(f"pmm{q}", [128, TG], F32)) for q in range(4)]
        psmall = st.enter_context(nc.psum_tensor("psmall", [128, 64], F32))

        chain = st.enter_context(nc.semaphore("chain"))
        block = st.enter_context(nc.Block())
        sch = Sched()
        dph, gph, cph = mkphases(sch, chain)

        dph([(sxl[:], xlast), (swp1[:], Wp1), (sbp1[:], bp1), (swp2[:], Wp2),
             (sbp2[:], bp2), (borow[:], bo_row), (onesr[:], ones_row),
             (onesc[:], onescol128), (siota[:], iota128), (sident[:], identf),
             (srb[:], revbig), (siota8[:], iota8), (b1all[:], b1r), (b2all[:], b2r)])
        cph("vector", lambda eng: eng.memset(sones8[:], 1.0))
        cph("vector", lambda eng: eng.memset(sones18[:], 1.0))
        cph("vector", lambda eng: eng.memset(seps[:], 1e-5))

        cph("scalar", lambda eng: eng.activation(sxls[:], sxl[:], AF.Square,
                                                 accum_out=sstat[:, 1:2]))
        cph("vector", lambda eng: eng.reduce_sum(sstat[:, 0:1], sxl[:], AX.X))
        cph("tensor", lambda eng: eng.matmul(psmall[0:1, 0:2], sones8[:], sstat[:],
                                             start=True, stop=True))
        cph("vector", lambda eng: eng.tensor_scalar(st11[:, 0:2], psmall[0:1, 0:2],
                                                    1.0 / 1024.0, None, ALU.mult))
        cph("vector", lambda eng: eng.tensor_tensor(st11[:, 2:3], st11[:, 0:1],
                                                    st11[:, 0:1], ALU.mult))
        cph("vector", lambda eng: eng.tensor_tensor(st11[:, 2:3], st11[:, 1:2],
                                                    st11[:, 2:3], ALU.subtract))
        cph("scalar", lambda eng: eng.activation(st11[:, 2:3], st11[:, 2:3],
                                                 AF.Sqrt, bias=seps[0:1, 0:1]))
        cph("vector", lambda eng: eng.reciprocal(st11[:, 2:3], st11[:, 2:3]))
        cph("vector", lambda eng: eng.tensor_copy(st11[:, 1:2], st11[:, 2:3]))
        cph("tensor", lambda eng: eng.matmul(psmall[0:8, 2:4], sones18[:], st11[:, 0:2],
                                             start=True, stop=True))
        cph("vector", lambda eng: eng.tensor_scalar(sxls[:], sxl[:], psmall[0:8, 2:3],
                                                    None, ALU.subtract))
        cph("vector", lambda eng: eng.tensor_scalar(sxls[:], sxls[:], psmall[0:8, 3:4],
                                                    None, ALU.mult))
        cph("tensor", lambda eng: eng.transpose(psmall[:, 8:16], sxls[0:8, 0:128], sident[0:8, 0:8]))
        cph("scalar", lambda eng: eng.copy(sxlT[:], psmall[:, 8:16]))

        def f_mmp1(eng):
            for k in range(8):
                last = eng.matmul(psmall[:, 16:17], swp1[:, 128 * k:128 * (k + 1)],
                                  sxlT[:, k:k + 1], start=(k == 0), stop=(k == 7))
            return last
        cph("tensor", f_mmp1)
        cph("scalar", lambda eng: eng.activation(sg1[:], psmall[:, 16:17], AF.Gelu,
                                                 bias=sbp1[:, 0:1]))
        cph("tensor", lambda eng: eng.matmul(psmall[0:8, 17:18], swp2[:], sg1[:],
                                             start=True, stop=True))
        cph("vector", lambda eng: eng.tensor_scalar(slogT[:], psmall[0:8, 17:18],
                                                    sbp2[:, 0:1], None, ALU.add))
        cph("tensor", lambda eng: eng.transpose(psmall[0:1, 24:32], slogT[0:8, 0:1], sident[0:8, 0:8]))
        cph("scalar", lambda eng: eng.copy(slog[:], psmall[0:1, 24:32]))
        cph("vector", lambda eng: eng.reduce_max(sm1[:], slog[:], AX.X))
        cph("vector", lambda eng: eng.scalar_tensor_tensor(sarg[:], slog[:], sm1[0:1, 0:1],
                                                           srb[:], ALU.is_equal, ALU.mult))
        cph("vector", lambda eng: eng.reduce_max(sm1[:], sarg[:], AX.X))
        cph("vector", lambda eng: eng.tensor_scalar(ssel[:], sm1[:], -1.0, 100.0,
                                                    ALU.mult, ALU.add))
        dph([(o_sel, ssel[:])])
        cph("tensor", lambda eng: eng.matmul(psmall[:, 32:33], onesc[:], ssel[:],
                                             start=True, stop=True))
        cph("vector", lambda eng: eng.tensor_copy(sselrep[:], psmall[:, 32:33]))

        cph("vector", lambda eng: eng.scalar_tensor_tensor(sidxf[:], sselrep[:], 1024.0,
                                                           siota[:], ALU.mult, ALU.add))
        for c in range(8):
            cph("vector", lambda eng, c=c: eng.tensor_scalar(sidx[:], sidxf[:],
                                                             float(128 * c), None, ALU.add))
            gph(lambda eng, c=c: eng.indirect_dma_start(
                stag[:], None, W1r, IndirectOffsetOnAxis(sidx[:, 0:1], 0)).then_inc(chain, 16))
            cph("vector", lambda eng, c=c: eng.tensor_copy(
                w1bf[:, c * F:(c + 1) * F], stag[:]))
        cph("vector", lambda eng: eng.scalar_tensor_tensor(sidxf[:], sselrep[:], 2048.0,
                                                           siota[:], ALU.mult, ALU.add))
        for c in range(16):
            cph("vector", lambda eng, c=c: eng.tensor_scalar(sidx[:], sidxf[:],
                                                             float(128 * c), None, ALU.add))
            gph(lambda eng, c=c: eng.indirect_dma_start(
                stag[:, 0:H], None, W2r, IndirectOffsetOnAxis(sidx[:, 0:1], 0)).then_inc(chain, 16))
            cph("vector", lambda eng, c=c: eng.tensor_copy(
                w2bf[:, c * H:(c + 1) * H], stag[:, 0:H]))
        cph("tensor", lambda eng: eng.matmul(psmall[0:8, 18:19], sones18[:], ssel[:],
                                             start=True, stop=True))
        cph("vector", lambda eng: eng.tensor_tensor(sonehot[:], siota8[:],
                                                    psmall[0:8, 18:19], ALU.is_equal))
        for c in range(8):
            cph("tensor", lambda eng, c=c: eng.matmul(
                pmm[0][0:1, 0:256], sonehot[:], b1all[:, 256 * c:256 * (c + 1)],
                start=True, stop=True))
            cph("scalar", lambda eng, c=c: eng.copy(
                b1row[0:1, 256 * c:256 * (c + 1)], pmm[0][0:1, 0:256]))
        for c in range(4):
            cph("tensor", lambda eng, c=c: eng.matmul(
                pmm[0][0:1, 0:256], sonehot[:], b2all[:, 256 * c:256 * (c + 1)],
                start=True, stop=True))
            cph("scalar", lambda eng, c=c: eng.copy(
                b2row[0:1, 256 * c:256 * (c + 1)], pmm[0][0:1, 0:256]))
        for c in range(8):
            dph([(stag[:, 0:H], Wo[128 * c:128 * (c + 1), :])])
            cph("vector", lambda eng, c=c: eng.tensor_copy(
                wobf[:, c * H:(c + 1) * H], stag[:, 0:H]))

        for g in range(NG):
            t0 = g * TG
            dph([(sx[:, j * H:(j + 1) * H], xin[t0 + 128 * j:t0 + 128 * (j + 1), :])
                 for j in range(NTT)])
            emit_ln(dph, cph, sx, shid, smu, ssq, srstd, sscr, NTT, H, seps[:, 0:1])

            for hc in range(NH):
                def f_tp(eng, hc=hc):
                    for j in range(NTT):
                        last = eng.transpose(pt[:, 128 * j:128 * (j + 1)],
                                             shid[:, j * H + 128 * hc:j * H + 128 * (hc + 1)],
                                             sident[:])
                    return last
                cph("tensor", f_tp)
                cph("scalar", lambda eng, hc=hc: eng.copy(
                    shTb[:, hc * TG:(hc + 1) * TG], pt[:, 0:TG]))

            for fc0 in range(0, 16, 4):
                def f_mm(eng, fc0=fc0):
                    for q in range(4):
                        fc = fc0 + q
                        pm = pmm[q]
                        for k in range(NH):
                            eng.matmul(pm[:], w1bf[:, k * F + 128 * fc:k * F + 128 * (fc + 1)],
                                       shTb[:, k * TG:(k + 1) * TG],
                                       start=(k == 0), stop=False)
                        last = eng.matmul(pm[:], b1row[0:1, 128 * fc:128 * (fc + 1)],
                                          onesr[:], start=False, stop=True)
                    return last
                cph("tensor", f_mm)

                def f_ge(eng, fc0=fc0):
                    for q in range(4):
                        fc = fc0 + q
                        last = eng.activation(sh1[:, fc * TG:(fc + 1) * TG],
                                              pmm[q][:], AF.Gelu)
                    return last
                cph("scalar", f_ge)

            for mc0 in range(0, 8, 4):
                def f_mm(eng, mc0=mc0):
                    for q in range(4):
                        mc = mc0 + q
                        pm = pmm[q]
                        for k in range(16):
                            eng.matmul(pm[:], w2bf[:, k * H + 128 * mc:k * H + 128 * (mc + 1)],
                                       sh1[:, k * TG:(k + 1) * TG],
                                       start=(k == 0), stop=False)
                        last = eng.matmul(pm[:], b2row[0:1, 128 * mc:128 * (mc + 1)],
                                          onesr[:], start=False, stop=True)
                    return last
                cph("tensor", f_mm)

                def f_ev(eng, mc0=mc0):
                    for q in range(4):
                        mc = mc0 + q
                        last = eng.copy(sr1[:, mc * TG:(mc + 1) * TG], pmm[q][:])
                    return last
                cph("scalar", f_ev)

            for mc0 in range(0, 8, 4):
                def f_mm(eng, mc0=mc0):
                    for q in range(4):
                        mc = mc0 + q
                        pm = pmm[q]
                        for k in range(8):
                            eng.matmul(pm[:], wobf[:, k * H + 128 * mc:k * H + 128 * (mc + 1)],
                                       sr1[:, k * TG:(k + 1) * TG],
                                       start=(k == 0), stop=False)
                        last = eng.matmul(pm[:], borow[0:1, 128 * mc:128 * (mc + 1)],
                                          onesr[:], start=False, stop=True)
                    return last
                cph("tensor", f_mm)

                def f_ev(eng, mc0=mc0):
                    for q in range(4):
                        mc = mc0 + q
                        last = eng.copy(so1[:, mc * TG:(mc + 1) * TG], pmm[q][:])
                    return last
                cph("scalar", f_ev)

            for j in range(NTT):
                def f_tp(eng, j=j):
                    for hc in range(NH):
                        last = eng.transpose(pt[:, 128 * hc:128 * (hc + 1)],
                                             so1[:, hc * TG + 128 * j:hc * TG + 128 * (j + 1)],
                                             sident[:])
                    return last
                cph("tensor", f_tp)
                cph("vector", lambda eng, j=j: eng.scalar_tensor_tensor(
                    sscr[:, 0:H], pt[:, 0:H], 0.15, sx[:, j * H:(j + 1) * H],
                    ALU.mult, ALU.add))
                cph("vector", lambda eng, j=j: eng.scalar_tensor_tensor(
                    sx[:, j * H:(j + 1) * H], shid[:, j * H:(j + 1) * H], 0.5,
                    sscr[:, 0:H], ALU.mult, ALU.add))

            dph([(xout[t0 + 128 * j:t0 + 128 * (j + 1), :], sx[:, j * H:(j + 1) * H])
                 for j in range(NTT)])

        sch.emit(nc, block, chain)
    return nc


def l0ffn_inmaps(x_full, ins, ntok):
    H, F = 1024, 2048
    common = {
        "Wp1": np.concatenate([ins["L0_ffn_Wp1"][128 * k:128 * (k + 1), :]
                               for k in range(8)], axis=1),
        "bp1": ins["L0_ffn_bp1"].reshape(128, 1),
        "Wp2": ins["L0_ffn_Wp2"],
        "bp2": ins["L0_ffn_bp2"].reshape(8, 1),
        "W1r": ins["L0_ffn_W1"].reshape(8 * H, F),
        "W2r": ins["L0_ffn_W2"].reshape(8 * F, H),
        "b1r": ins["L0_ffn_b1"],
        "b2r": ins["L0_ffn_b2"],
        "Wo": ins["L0_ffn_Wo"],
        "bo_row": ins["L0_ffn_bo"].reshape(1, H),
        "identf": np.eye(128, dtype=np.float32),
        "ones_row": np.ones((1, 256), dtype=np.float32),
        "onescol128": np.ones((1, 128), dtype=np.float32),
        "iota128": np.arange(128, dtype=np.float32).reshape(128, 1),
        "revbig": (100.0 - np.arange(8, dtype=np.float32)).reshape(1, 8),
        "iota8": np.arange(8, dtype=np.float32).reshape(8, 1),
    }
    common = {k: np.ascontiguousarray(v, dtype=np.float32) for k, v in common.items()}
    maps = []
    for c in range(NCORES):
        b, h = c // 2, c % 2
        m = dict(common)
        m["xin"] = np.ascontiguousarray(x_full[b, h * ntok:(h + 1) * ntok, :])
        m["xlast"] = np.ascontiguousarray(x_full[b, SEQ - 1, :].reshape(8, 128))
        maps.append(m)
    return maps


def run_l0ffn(x_full, ins):
    nc = cached('build_l0ffn', build_l0ffn, TLOC)
    res = run_bass_kernel_spmd(nc, l0ffn_inmaps(x_full, ins, TLOC),
                               core_ids=list(range(NCORES)))
    out = np.empty((BATCH, SEQ, DM), np.float32)
    for c in range(NCORES):
        b, h = c // 2, c % 2
        out[b, h * TLOC:(h + 1) * TLOC, :] = res.results[c]["xout"]
    sels = [res.results[2 * b]["o_sel"][0, 0] for b in range(BATCH)]
    return out, sels


def build_l0attn_a(ntok):
    nc = bass.Bass("TRN2", target_bir_lowering=False)
    TG = 256
    NG = ntok // TG
    NTT = TG // 128
    H, J = 1024, 3072
    NH, NJ = H // 128, J // 128

    xin = nc.dram_tensor("xin", [ntok, H], F32, kind="ExternalInput").ap()
    WinT = nc.dram_tensor("WinT", [H, J], F32, kind="ExternalInput").ap()
    grep = nc.dram_tensor("grep", [128, H], F32, kind="ExternalInput").ap()
    brep = nc.dram_tensor("brep", [128, H], F32, kind="ExternalInput").ap()
    identf = nc.dram_tensor("identf", [128, 128], F32, kind="ExternalInput").ap()
    xz = nc.dram_tensor("xz", [J, ntok], F32, kind="ExternalOutput").ap()

    with contextlib.ExitStack() as st:
        def sb(name, shape, dt=F32):
            return st.enter_context(nc.sbuf_tensor(name, shape, dt))
        winbf = sb("winbf", [128, NH * J], BF16)
        stag = sb("stag", [128, J], F32)
        sgrep = sb("sgrep", [128, H], F32)
        sbrep = sb("sbrep", [128, H], F32)
        sident = sb("sident", [128, 128], F32)
        seps = sb("seps", [128, 1], F32)
        sx = sb("sx", [128, H * NTT], F32)
        shid = sb("shid", [128, H * NTT], F32)
        smu = sb("smu", [128, NTT], F32)
        ssq = sb("ssq", [128, NTT], F32)
        srstd = sb("srstd", [128, NTT], F32)
        sscr = sb("sscr", [128, H], F32)
        shTb = sb("shTb", [128, NH * TG], BF16)
        sxz = sb("sxz", [128, 4 * TG], F32)

        pt = st.enter_context(nc.psum_tensor("pt", [128, 512], F32))
        pmm = [st.enter_context(nc.psum_tensor(f"pmm{q}", [128, TG], F32)) for q in range(4)]

        chain = st.enter_context(nc.semaphore("chain"))
        block = st.enter_context(nc.Block())
        sch = Sched()
        dph, gph, cph = mkphases(sch, chain)

        dph([(sgrep[:], grep), (sbrep[:], brep), (sident[:], identf)])
        cph("vector", lambda eng: eng.memset(seps[:], 1e-5))
        for k in range(NH):
            dph([(stag[:], WinT[128 * k:128 * (k + 1), :])])
            cph("vector", lambda eng, k=k: eng.tensor_copy(winbf[:, k * J:(k + 1) * J], stag[:]))

        for g in range(NG):
            t0 = g * TG
            dph([(sx[:, j * H:(j + 1) * H], xin[t0 + 128 * j:t0 + 128 * (j + 1), :])
                 for j in range(NTT)])
            emit_ln(dph, cph, sx, shid, smu, ssq, srstd, sscr, NTT, H, seps[:, 0:1])

            def f_g(eng):
                for j in range(NTT):
                    last = eng.tensor_tensor(shid[:, j * H:(j + 1) * H],
                                             shid[:, j * H:(j + 1) * H], sgrep[:], ALU.mult)
                return last
            cph("vector", f_g)

            def f_b(eng):
                for j in range(NTT):
                    last = eng.tensor_tensor(shid[:, j * H:(j + 1) * H],
                                             shid[:, j * H:(j + 1) * H], sbrep[:], ALU.add)
                return last
            cph("vector", f_b)

            for hc in range(NH):
                def f_tp(eng, hc=hc):
                    for j in range(NTT):
                        last = eng.transpose(pt[:, 128 * j:128 * (j + 1)],
                                             shid[:, j * H + 128 * hc:j * H + 128 * (hc + 1)],
                                             sident[:])
                    return last
                cph("tensor", f_tp)
                cph("scalar", lambda eng, hc=hc: eng.copy(
                    shTb[:, hc * TG:(hc + 1) * TG], pt[:, 0:TG]))

            for jc0 in range(0, NJ, 4):
                def f_mm(eng, jc0=jc0):
                    for q in range(4):
                        jc = jc0 + q
                        for k in range(NH):
                            last = eng.matmul(pmm[q][:],
                                              winbf[:, k * J + 128 * jc:k * J + 128 * (jc + 1)],
                                              shTb[:, k * TG:(k + 1) * TG],
                                              start=(k == 0), stop=(k == NH - 1))
                    return last
                cph("tensor", f_mm)

                def f_ev(eng, jc0=jc0):
                    for q in range(4):
                        last = eng.tensor_scalar(sxz[:, q * TG:(q + 1) * TG], pmm[q][:],
                                                 -5.0, 5.0, ALU.max, ALU.min)
                    return last
                cph("vector", f_ev)
                dph([(xz[128 * (jc0 + q):128 * (jc0 + q + 1), t0:t0 + TG],
                      sxz[:, q * TG:(q + 1) * TG]) for q in range(4)])

        sch.emit(nc, block, chain)
    return nc


def build_l0attn_b(iloc, L):
    nc = bass.Bass("TRN2", target_bir_lowering=False)
    NI = iloc // 128
    D = 8
    TC = 1024
    NTC = L // TC

    xs = nc.dram_tensor("xs", [iloc, L], F32, kind="ExternalInput").ap()
    gate = nc.dram_tensor("gate", [iloc, L], F32, kind="ExternalInput").ap()
    alog = nc.dram_tensor("alog", [iloc, D], F32, kind="ExternalInput").ap()
    bmat = nc.dram_tensor("bmat", [iloc, D], F32, kind="ExternalInput").ap()
    cmat = nc.dram_tensor("cmat", [iloc, D], F32, kind="ExternalInput").ap()
    sy_out = nc.dram_tensor("sy", [iloc, L], F32, kind="ExternalOutput").ap()

    with contextlib.ExitStack() as st:
        def sb(name, shape, dt=F32):
            return st.enter_context(nc.sbuf_tensor(name, shape, dt))
        sdec = sb("sdec", [128, NI * D], F32)
        sb01 = sb("sb01", [128, NI * D], F32)
        scc = sb("scc", [128, NI * D], F32)
        sstate = sb("sstate", [128, NI * D], F32)
        sxs = sb("sxs", [128, NI * TC], F32)
        sgt = sb("sgt", [128, NI * TC], F32)
        sy = sb("syt", [128, NI * TC], F32)
        sh = sb("sh", [128, NI * TC], F32)
        su = sb("su", [128, NI * TC], F32)

        chain = st.enter_context(nc.semaphore("chain"))
        block = st.enter_context(nc.Block())
        sch = Sched()
        dph, gph, cph = mkphases(sch, chain)

        dph([(sdec[:, it * D:(it + 1) * D], alog[128 * it:128 * (it + 1), :])
             for it in range(NI)])
        dph([(sb01[:, it * D:(it + 1) * D], bmat[128 * it:128 * (it + 1), :])
             for it in range(NI)])
        dph([(scc[:, it * D:(it + 1) * D], cmat[128 * it:128 * (it + 1), :])
             for it in range(NI)])
        cph("vector", lambda eng: eng.tensor_scalar(sdec[:], sdec[:], -5.0, 0.0,
                                                    ALU.max, ALU.min))
        cph("scalar", lambda eng: eng.activation(sdec[:], sdec[:], AF.Exp))
        cph("vector", lambda eng: eng.tensor_scalar(sdec[:], sdec[:], -1.0, None, ALU.mult))
        cph("vector", lambda eng: eng.tensor_scalar(sdec[:], sdec[:], -2.0, -0.01,
                                                    ALU.max, ALU.min))
        cph("vector", lambda eng: eng.tensor_scalar(sdec[:], sdec[:], 0.9, None, ALU.mult))
        cph("vector", lambda eng: eng.tensor_scalar(sb01[:], sb01[:], 0.1, None, ALU.mult))
        cph("vector", lambda eng: eng.memset(sstate[:], 0.0))

        for tc in range(NTC):
            c0 = tc * TC
            dph([(sxs[:, it * TC:(it + 1) * TC],
                  xs[128 * it:128 * (it + 1), c0:c0 + TC]) for it in range(NI)])
            dph([(sgt[:, it * TC:(it + 1) * TC],
                  gate[128 * it:128 * (it + 1), c0:c0 + TC]) for it in range(NI)])
            cph("vector", lambda eng: eng.memset(sy[:], 0.0))
            for d in range(D):
                def f_u(eng, d=d):
                    for it in range(NI):
                        last = eng.tensor_scalar(su[:, it * TC:(it + 1) * TC],
                                                 sxs[:, it * TC:(it + 1) * TC],
                                                 sb01[:, it * D + d:it * D + d + 1],
                                                 None, ALU.mult)
                    return last
                cph("vector", f_u)

                def f_scan(eng, d=d):
                    for it in range(NI):
                        last = eng.tensor_tensor_scan(
                            sh[:, it * TC:(it + 1) * TC],
                            sdec[:, it * D + d:it * D + d + 1].broadcast_to([128, TC]),
                            su[:, it * TC:(it + 1) * TC],
                            sstate[:, it * D + d:it * D + d + 1],
                            ALU.mult, ALU.add)
                    return last
                cph("vector", f_scan)

                def f_acc(eng, d=d):
                    for it in range(NI):
                        eng.tensor_copy(sstate[:, it * D + d:it * D + d + 1],
                                        sh[:, (it + 1) * TC - 1:(it + 1) * TC])
                        last = eng.scalar_tensor_tensor(
                            sy[:, it * TC:(it + 1) * TC],
                            sh[:, it * TC:(it + 1) * TC],
                            scc[:, it * D + d:it * D + d + 1],
                            sy[:, it * TC:(it + 1) * TC], ALU.mult, ALU.add)
                    return last
                cph("vector", f_acc)

            cph("vector", lambda eng: eng.tensor_scalar(sy[:], sy[:], -5.0, 5.0,
                                                        ALU.max, ALU.min))
            cph("scalar", lambda eng: eng.activation(sgt[:], sgt[:], AF.Tanh))
            cph("vector", lambda eng: eng.tensor_tensor(sy[:], sy[:], sgt[:], ALU.mult))
            dph([(sy_out[128 * it:128 * (it + 1), c0:c0 + TC],
                  sy[:, it * TC:(it + 1) * TC]) for it in range(NI)])

        sch.emit(nc, block, chain)
    return nc


def build_l0attn_c(ntok):
    nc = bass.Bass("TRN2", target_bir_lowering=False)
    TG = 256
    NG = ntok // TG
    NTT = TG // 128
    H, I = 1024, 1536
    NH, NIc = H // 128, I // 128

    syin = nc.dram_tensor("syin", [I, ntok], F32, kind="ExternalInput").ap()
    WoutT = nc.dram_tensor("WoutT", [I, H], F32, kind="ExternalInput").ap()
    x1in = nc.dram_tensor("x1in", [ntok, H], F32, kind="ExternalInput").ap()
    grep = nc.dram_tensor("grep", [128, H], F32, kind="ExternalInput").ap()
    brep = nc.dram_tensor("brep", [128, H], F32, kind="ExternalInput").ap()
    identf = nc.dram_tensor("identf", [128, 128], F32, kind="ExternalInput").ap()
    x2out = nc.dram_tensor("x2out", [ntok, H], F32, kind="ExternalOutput").ap()

    with contextlib.ExitStack() as st:
        def sb(name, shape, dt=F32):
            return st.enter_context(nc.sbuf_tensor(name, shape, dt))
        woutbf = sb("woutbf", [128, NIc * H], BF16)
        stag = sb("stag", [128, H], F32)
        sgrep = sb("sgrep", [128, H], F32)
        sbrep = sb("sbrep", [128, H], F32)
        sident = sb("sident", [128, 128], F32)
        seps = sb("seps", [128, 1], F32)
        ssy = sb("ssy", [128, NIc * TG], F32)
        ssyb = sb("ssyb", [128, NIc * TG], BF16)
        sot = sb("sot", [128, NH * TG], F32)
        sx = sb("sx", [128, H * NTT], F32)
        sat = sb("sat", [128, H * NTT], F32)
        smu = sb("smu", [128, NTT], F32)
        ssq = sb("ssq", [128, NTT], F32)
        srstd = sb("srstd", [128, NTT], F32)
        sscr = sb("sscr", [128, H], F32)

        pt = st.enter_context(nc.psum_tensor("pt", [128, 1024], F32))
        pmm = [st.enter_context(nc.psum_tensor(f"pmm{q}", [128, TG], F32)) for q in range(4)]

        chain = st.enter_context(nc.semaphore("chain"))
        block = st.enter_context(nc.Block())
        sch = Sched()
        dph, gph, cph = mkphases(sch, chain)

        dph([(sgrep[:], grep), (sbrep[:], brep), (sident[:], identf)])
        cph("vector", lambda eng: eng.memset(seps[:], 1e-5))
        for k in range(NIc):
            dph([(stag[:], WoutT[128 * k:128 * (k + 1), :])])
            cph("vector", lambda eng, k=k: eng.tensor_copy(woutbf[:, k * H:(k + 1) * H], stag[:]))

        for g in range(NG):
            t0 = g * TG
            dph([(ssy[:, k * TG:(k + 1) * TG], syin[128 * k:128 * (k + 1), t0:t0 + TG])
                 for k in range(NIc)])
            cph("vector", lambda eng: eng.tensor_copy(ssyb[:], ssy[:]))

            for mc0 in range(0, NH, 4):
                def f_mm(eng, mc0=mc0):
                    for q in range(4):
                        mc = mc0 + q
                        for k in range(NIc):
                            last = eng.matmul(pmm[q][:],
                                              woutbf[:, k * H + 128 * mc:k * H + 128 * (mc + 1)],
                                              ssyb[:, k * TG:(k + 1) * TG],
                                              start=(k == 0), stop=(k == NIc - 1))
                    return last
                cph("tensor", f_mm)

                def f_ev(eng, mc0=mc0):
                    for q in range(4):
                        mc = mc0 + q
                        last = eng.tensor_scalar(sot[:, mc * TG:(mc + 1) * TG], pmm[q][:],
                                                 -5.0, 5.0, ALU.max, ALU.min)
                    return last
                cph("vector", f_ev)

            for j in range(NTT):
                def f_tp(eng, j=j):
                    for hc in range(NH):
                        last = eng.transpose(pt[:, 128 * hc:128 * (hc + 1)],
                                             sot[:, hc * TG + 128 * j:hc * TG + 128 * (j + 1)],
                                             sident[:])
                    return last
                cph("tensor", f_tp)
                cph("scalar", lambda eng, j=j: eng.copy(sat[:, j * H:(j + 1) * H], pt[:, 0:H]))

            emit_ln(dph, cph, sat, sat, smu, ssq, srstd, sscr, NTT, H, seps[:, 0:1])
            dph([(sx[:, j * H:(j + 1) * H], x1in[t0 + 128 * j:t0 + 128 * (j + 1), :])
                 for j in range(NTT)])

            def f_g(eng):
                for j in range(NTT):
                    last = eng.tensor_tensor(sat[:, j * H:(j + 1) * H],
                                             sat[:, j * H:(j + 1) * H], sgrep[:], ALU.mult)
                return last
            cph("vector", f_g)

            def f_b(eng):
                for j in range(NTT):
                    last = eng.tensor_tensor(sat[:, j * H:(j + 1) * H],
                                             sat[:, j * H:(j + 1) * H], sbrep[:], ALU.add)
                return last
            cph("vector", f_b)

            def f_res(eng):
                for j in range(NTT):
                    last = eng.tensor_tensor(sx[:, j * H:(j + 1) * H],
                                             sx[:, j * H:(j + 1) * H],
                                             sat[:, j * H:(j + 1) * H], ALU.add)
                return last
            cph("vector", f_res)
            dph([(x2out[t0 + 128 * j:t0 + 128 * (j + 1), :], sx[:, j * H:(j + 1) * H])
                 for j in range(NTT)])

        sch.emit(nc, block, chain)
    return nc


def run_l0attn(x1_full, ins):
    H, I2, I = 1024, 3072, 1536
    nc = cached('build_l0attn_a', build_l0attn_a, TLOC)
    common = {
        "WinT": np.ascontiguousarray(ins["L0_attn_W_in"].T),
        "grep": np.ascontiguousarray(np.broadcast_to(ins["L0_attn_ln_in_g"], (128, H))),
        "brep": np.ascontiguousarray(np.broadcast_to(ins["L0_attn_ln_in_b"], (128, H))),
        "identf": np.eye(128, dtype=np.float32),
    }
    maps = []
    for c in range(NCORES):
        b, hh = c // 2, c % 2
        m = dict(common)
        m["xin"] = np.ascontiguousarray(x1_full[b, hh * TLOC:(hh + 1) * TLOC, :])
        maps.append(m)
    res = run_bass_kernel_spmd(nc, maps, core_ids=list(range(NCORES)))
    xz = np.empty((BATCH, I2, SEQ), np.float32)
    for c in range(NCORES):
        b, hh = c // 2, c % 2
        xz[b, :, hh * TLOC:(hh + 1) * TLOC] = res.results[c]["xz"]

    iloc = I // 2
    nc = cached('l0b', build_l0attn_b, iloc, SEQ)
    maps = []
    for c in range(NCORES):
        b, ih = c // 2, c % 2
        sl = slice(ih * iloc, (ih + 1) * iloc)
        maps.append({
            "xs": np.ascontiguousarray(xz[b, sl.start:sl.stop, :]),
            "gate": np.ascontiguousarray(xz[b, I + sl.start:I + sl.stop, :]),
            "alog": np.ascontiguousarray(ins["L0_attn_A_log"][sl]),
            "bmat": np.ascontiguousarray(ins["L0_attn_B"][sl]),
            "cmat": np.ascontiguousarray(ins["L0_attn_C"][sl]),
        })
    res = run_bass_kernel_spmd(nc, maps, core_ids=list(range(NCORES)))
    sy = np.empty((BATCH, I, SEQ), np.float32)
    for c in range(NCORES):
        b, ih = c // 2, c % 2
        sy[b, ih * iloc:(ih + 1) * iloc, :] = res.results[c]["sy"]

    nc = cached('build_l0attn_c', build_l0attn_c, TLOC)
    common = {
        "WoutT": np.ascontiguousarray(ins["L0_attn_W_out"].T),
        "grep": np.ascontiguousarray(np.broadcast_to(ins["L0_attn_ln_out_g"], (128, H))),
        "brep": np.ascontiguousarray(np.broadcast_to(ins["L0_attn_ln_out_b"], (128, H))),
        "identf": np.eye(128, dtype=np.float32),
    }
    maps = []
    for c in range(NCORES):
        b, hh = c // 2, c % 2
        m = dict(common)
        m["syin"] = np.ascontiguousarray(sy[b, :, hh * TLOC:(hh + 1) * TLOC])
        m["x1in"] = np.ascontiguousarray(x1_full[b, hh * TLOC:(hh + 1) * TLOC, :])
        maps.append(m)
    res = run_bass_kernel_spmd(nc, maps, core_ids=list(range(NCORES)))
    x2 = np.empty((BATCH, SEQ, DM), np.float32)
    for c in range(NCORES):
        b, hh = c // 2, c % 2
        x2[b, hh * TLOC:(hh + 1) * TLOC, :] = res.results[c]["x2out"]
    return x2, xz, sy


QTHR = [0.25, 0.75, 1.25, 1.75, 2.5, 2.5, 3.5, 3.5, 5.0, 5.0, 5.0, 5.0]


def emit_quant(cph, src, dst_bf, v, acc_bf, sgn_bf, scl, sclh, W):
    nb = W // 16

    def r3(ap, W=W):
        return ap.rearrange("p (g w) -> p g w", w=16)

    cph("vector", lambda eng: eng.reduce_max(scl[:, 0:nb], r3(src), AX.X,
                                             apply_absolute_value=True))
    cph("vector", lambda eng: eng.tensor_scalar(scl[:, 0:nb], scl[:, 0:nb],
                                                1.0 / 6.0, 1e-12, ALU.mult, ALU.max))
    cph("vector", lambda eng: eng.reciprocal(sclh[:, 0:nb], scl[:, 0:nb]))
    cph("scalar", lambda eng: eng.activation(v[:, 0:W], src, AF.Abs))
    cph("vector", lambda eng: eng.tensor_tensor(
        r3(v[:, 0:W]), r3(v[:, 0:W]),
        sclh[:, 0:nb].unsqueeze(2).broadcast_to([128, nb, 16]), ALU.mult))
    cph("vector", lambda eng: eng.tensor_scalar(acc_bf[:, 0:W], v[:, 0:W],
                                                QTHR[0], None, ALU.is_gt))
    for thr in QTHR[1:]:
        cph("vector", lambda eng, thr=thr: eng.scalar_tensor_tensor(
            acc_bf[:, 0:W], v[:, 0:W], thr, acc_bf[:, 0:W], ALU.is_gt, ALU.add))
    cph("scalar", lambda eng: eng.activation(sgn_bf[:, 0:W], src, AF.Sign))
    cph("vector", lambda eng: eng.tensor_tensor(acc_bf[:, 0:W], acc_bf[:, 0:W],
                                                sgn_bf[:, 0:W], ALU.mult))
    cph("vector", lambda eng: eng.tensor_scalar(sclh[:, 0:nb], scl[:, 0:nb],
                                                0.5, None, ALU.mult))
    cph("vector", lambda eng: eng.tensor_tensor(
        r3(dst_bf[:, 0:W]), r3(acc_bf[:, 0:W]),
        sclh[:, 0:nb].unsqueeze(2).broadcast_to([128, nb, 16]), ALU.mult))


def build_l1ffn_a(ntok):
    nc = bass.Bass("TRN2", target_bir_lowering=False)
    TG = 256
    NG = ntok // TG
    NTT = TG // 128
    H = 1024
    NH = H // 128

    xin = nc.dram_tensor("xin", [ntok, H], F32, kind="ExternalInput").ap()
    identb = nc.dram_tensor("identb", [128, 128], BF16, kind="ExternalInput").ap()
    xqT = nc.dram_tensor("xqT", [H, ntok], BF16, kind="ExternalOutput").ap()

    with contextlib.ExitStack() as st:
        def sb(name, shape, dt=F32):
            return st.enter_context(nc.sbuf_tensor(name, shape, dt))
        sidb = sb("sidb", [128, 128], BF16)
        seps = sb("seps", [128, 1], F32)
        sx = sb("sx", [128, H * NTT], F32)
        shid = sb("shid", [128, H * NTT], F32)
        sv = sb("sv", [128, H * NTT], F32)
        sacc = sb("sacc", [128, H * NTT], BF16)
        ssgn = sb("ssgn", [128, H * NTT], BF16)
        sscl = sb("sscl", [128, H * NTT // 16], F32)
        sscl2 = sb("sscl2", [128, H * NTT // 16], F32)
        smu = sb("smu", [128, NTT], F32)
        ssq = sb("ssq", [128, NTT], F32)
        srstd = sb("srstd", [128, NTT], F32)
        sscr = sb("sscr", [128, H], F32)
        sxqT = sb("sxqT", [128, NH * TG], BF16)
        ptb = st.enter_context(nc.psum_tensor("ptb", [128, 512], BF16))

        chain = st.enter_context(nc.semaphore("chain"))
        block = st.enter_context(nc.Block())
        sch = Sched()
        dph, gph, cph = mkphases(sch, chain)

        dph([(sidb[:], identb)])
        cph("vector", lambda eng: eng.memset(seps[:], 1e-5))

        for g in range(NG):
            t0 = g * TG
            dph([(sx[:, j * H:(j + 1) * H], xin[t0 + 128 * j:t0 + 128 * (j + 1), :])
                 for j in range(NTT)])
            emit_ln(dph, cph, sx, shid, smu, ssq, srstd, sscr, NTT, H, seps[:, 0:1])
            emit_quant(cph, shid[:, 0:H * NTT], sacc, sv, sacc, ssgn, sscl, sscl2, H * NTT)

            for hc in range(NH):
                def f_tp(eng, hc=hc):
                    for j in range(NTT):
                        last = eng.transpose(ptb[:, 128 * j:128 * (j + 1)],
                                             sacc[:, j * H + 128 * hc:j * H + 128 * (hc + 1)],
                                             sidb[:])
                    return last
                cph("tensor", f_tp)
                cph("scalar", lambda eng, hc=hc: eng.copy(
                    sxqT[:, hc * TG:(hc + 1) * TG], ptb[:, 0:TG]))
            dph([(xqT[128 * hc:128 * (hc + 1), t0:t0 + TG],
                  sxqT[:, hc * TG:(hc + 1) * TG]) for hc in range(NH)])

        sch.emit(nc, block, chain)
    return nc


def build_l1ffn_b(ntok):
    nc = bass.Bass("TRN2", target_bir_lowering=False)
    TG = 256
    NG = ntok // TG
    NTT = TG // 128
    H, I = 1024, 2816
    NH, NI = H // 128, I // 128

    xqTin = nc.dram_tensor("xqTin", [H, ntok], BF16, kind="ExternalInput").ap()
    wg = nc.dram_tensor("wg", [I, H], F32, kind="ExternalInput").ap()
    wu = nc.dram_tensor("wu", [I, H], F32, kind="ExternalInput").ap()
    identb = nc.dram_tensor("identb", [128, 128], BF16, kind="ExternalInput").ap()
    identf = nc.dram_tensor("identf", [128, 128], F32, kind="ExternalInput").ap()
    hq_out = nc.dram_tensor("hq", [ntok, I], BF16, kind="ExternalOutput").ap()

    with contextlib.ExitStack() as st:
        def sb(name, shape, dt=F32):
            return st.enter_context(nc.sbuf_tensor(name, shape, dt))
        sidb = sb("sidb", [128, 128], BF16)
        sidf = sb("sidf", [128, 128], F32)
        wgq = sb("wgq", [128, NH * I], BF16)
        wuq = sb("wuq", [128, NH * I], BF16)
        stagw = sb("stagw", [128, 2 * H], F32)
        swq = sb("swq", [128, 2 * H], BF16)
        sv = sb("sv", [128, NTT * I], F32)
        sacc = sb("sacc", [128, NTT * I], BF16)
        ssgn = sb("ssgn", [128, NTT * I], BF16)
        sscl = sb("sscl", [128, NTT * I // 16], F32)
        sscl2 = sb("sscl2", [128, NTT * I // 16], F32)
        sxq = sb("sxq", [128, NH * TG], BF16)
        sup = sb("sup", [128, NI * TG], F32)
        shtok = sb("shtok", [128, NTT * I], F32)

        ptb = st.enter_context(nc.psum_tensor("ptb", [128, 512], BF16))
        pmm = [st.enter_context(nc.psum_tensor(f"pmm{q}", [128, TG], F32)) for q in range(4)]

        chain = st.enter_context(nc.semaphore("chain"))
        block = st.enter_context(nc.Block())
        sch = Sched()
        dph, gph, cph = mkphases(sch, chain)

        dph([(sidb[:], identb), (sidf[:], identf)])

        for (wsrc, wdst) in ((wg, wgq), (wu, wuq)):
            for nt0 in range(0, NI, 2):
                nrow = min(2, NI - nt0)
                W = nrow * H
                dph([(stagw[:, r * H:(r + 1) * H],
                      wsrc[128 * (nt0 + r):128 * (nt0 + r + 1), :]) for r in range(nrow)])
                emit_quant(cph, stagw[:, 0:W], swq, sv, sacc, ssgn, sscl, sscl2, W)
                for r in range(nrow):
                    for half in range(2):
                        def f_tp2(eng, r=r, nt0=nt0, half=half):
                            for kk in range(4):
                                k = half * 4 + kk
                                last = eng.transpose(ptb[:, 128 * kk:128 * (kk + 1)],
                                                     swq[:, r * H + 128 * k:r * H + 128 * (k + 1)],
                                                     sidb[:])
                            return last
                        cph("tensor", f_tp2)

                        def f_ev(eng, r=r, nt0=nt0, half=half, wdst=wdst):
                            for kk in range(4):
                                k = half * 4 + kk
                                last = eng.copy(
                                    wdst[:, k * I + 128 * (nt0 + r):k * I + 128 * (nt0 + r + 1)],
                                    ptb[:, 128 * kk:128 * (kk + 1)])
                            return last
                        cph("scalar", f_ev)

        for g in range(NG):
            t0 = g * TG
            dph([(sxq[:, k * TG:(k + 1) * TG], xqTin[128 * k:128 * (k + 1), t0:t0 + TG])
                 for k in range(NH)])

            for nc0 in range(0, NI, 4):
                nn = min(4, NI - nc0)

                def f_mmu(eng, nc0=nc0, nn=nn):
                    for q in range(nn):
                        mc = nc0 + q
                        for k in range(NH):
                            last = eng.matmul(pmm[q][:],
                                              wuq[:, k * I + 128 * mc:k * I + 128 * (mc + 1)],
                                              sxq[:, k * TG:(k + 1) * TG],
                                              start=(k == 0), stop=(k == NH - 1))
                    return last
                cph("tensor", f_mmu)

                def f_evu(eng, nc0=nc0, nn=nn):
                    for q in range(nn):
                        mc = nc0 + q
                        last = eng.copy(sup[:, mc * TG:(mc + 1) * TG], pmm[q][:])
                    return last
                cph("scalar", f_evu)

            for nc0 in range(0, NI, 4):
                nn = min(4, NI - nc0)

                def f_mmg(eng, nc0=nc0, nn=nn):
                    for q in range(nn):
                        mc = nc0 + q
                        for k in range(NH):
                            last = eng.matmul(pmm[q][:],
                                              wgq[:, k * I + 128 * mc:k * I + 128 * (mc + 1)],
                                              sxq[:, k * TG:(k + 1) * TG],
                                              start=(k == 0), stop=(k == NH - 1))
                    return last
                cph("tensor", f_mmg)

                def f_evg(eng, nc0=nc0, nn=nn):
                    for q in range(nn):
                        mc = nc0 + q
                        last = eng.activation(sv[:, mc * TG:(mc + 1) * TG],
                                              pmm[q][:], AF.Silu)
                    return last
                cph("scalar", f_evg)

                def f_h(eng, nc0=nc0, nn=nn):
                    for q in range(nn):
                        mc = nc0 + q
                        last = eng.tensor_tensor(sup[:, mc * TG:(mc + 1) * TG],
                                                 sup[:, mc * TG:(mc + 1) * TG],
                                                 sv[:, mc * TG:(mc + 1) * TG], ALU.mult)
                    return last
                cph("vector", f_h)

            for j in range(NTT):
                for c0 in range(0, NI, 4):
                    nn = min(4, NI - c0)

                    def f_tp(eng, j=j, c0=c0, nn=nn):
                        for kk in range(nn):
                            ic = c0 + kk
                            last = eng.transpose(pmm[kk][:, 0:128],
                                                 sup[:, ic * TG + 128 * j:ic * TG + 128 * (j + 1)],
                                                 sidf[:])
                        return last
                    cph("tensor", f_tp)

                    def f_ev(eng, j=j, c0=c0, nn=nn):
                        for kk in range(nn):
                            ic = c0 + kk
                            last = eng.copy(shtok[:, j * I + 128 * ic:j * I + 128 * (ic + 1)],
                                            pmm[kk][:, 0:128])
                        return last
                    cph("scalar", f_ev)

            emit_quant(cph, shtok[:, 0:NTT * I], sacc, sv, sacc, ssgn, sscl, sscl2, NTT * I)
            dph([(hq_out[t0 + 128 * j:t0 + 128 * (j + 1), :], sacc[:, j * I:(j + 1) * I])
                 for j in range(NTT)])

        sch.emit(nc, block, chain)
    return nc


def build_l1ffn_d(ntok):
    nc = bass.Bass("TRN2", target_bir_lowering=False)
    TG = 256
    NG = ntok // TG
    NTT = TG // 128
    H, I = 1024, 2816
    NH, NI = H // 128, I // 128

    hqT = nc.dram_tensor("hqT", [I, ntok], BF16, kind="ExternalInput").ap()
    wd = nc.dram_tensor("wd", [H, I], F32, kind="ExternalInput").ap()
    xres = nc.dram_tensor("xres", [ntok, H], F32, kind="ExternalInput").ap()
    identf = nc.dram_tensor("identf", [128, 128], F32, kind="ExternalInput").ap()
    identb = nc.dram_tensor("identb", [128, 128], BF16, kind="ExternalInput").ap()
    xout = nc.dram_tensor("xout", [ntok, H], F32, kind="ExternalOutput").ap()

    with contextlib.ExitStack() as st:
        def sb(name, shape, dt=F32):
            return st.enter_context(nc.sbuf_tensor(name, shape, dt))
        sidf = sb("sidf", [128, 128], F32)
        sidb = sb("sidb", [128, 128], BF16)
        wdq = sb("wdq", [128, NI * H], BF16)
        stagw = sb("stagw", [128, 2 * I], F32)
        swq = sb("swq", [128, 2 * I], BF16)
        sv = sb("sv", [128, 2 * I], F32)
        sacc = sb("sacc", [128, 2 * I], BF16)
        ssgn = sb("ssgn", [128, 2 * I], BF16)
        sscl = sb("sscl", [128, 2 * I // 16], F32)
        sscl2 = sb("sscl2", [128, 2 * I // 16], F32)
        shq = sb("shq", [128, NI * TG], BF16)
        sot = sb("sot", [128, NH * TG], F32)
        sx = sb("sx", [128, NTT * H], F32)

        ptb = st.enter_context(nc.psum_tensor("ptb", [128, 512], BF16))
        pt = st.enter_context(nc.psum_tensor("pt", [128, 1024], F32))
        pmm = [st.enter_context(nc.psum_tensor(f"pmm{q}", [128, TG], F32)) for q in range(4)]

        chain = st.enter_context(nc.semaphore("chain"))
        block = st.enter_context(nc.Block())
        sch = Sched()
        dph, gph, cph = mkphases(sch, chain)

        dph([(sidf[:], identf), (sidb[:], identb)])

        for nt0 in range(0, NH, 2):
            W = 2 * I
            dph([(stagw[:, r * I:(r + 1) * I], wd[128 * (nt0 + r):128 * (nt0 + r + 1), :])
                 for r in range(2)])
            emit_quant(cph, stagw[:, 0:W], swq, sv, sacc, ssgn, sscl, sscl2, W)
            for r in range(2):
                for c0 in range(0, NI, 4):
                    nn = min(4, NI - c0)

                    def f_tp(eng, r=r, c0=c0, nn=nn):
                        for kk in range(nn):
                            k = c0 + kk
                            last = eng.transpose(ptb[:, 128 * kk:128 * (kk + 1)],
                                                 swq[:, r * I + 128 * k:r * I + 128 * (k + 1)],
                                                 sidb[:])
                        return last
                    cph("tensor", f_tp)

                    def f_ev(eng, r=r, c0=c0, nn=nn, nt0=nt0):
                        for kk in range(nn):
                            k = c0 + kk
                            last = eng.copy(
                                wdq[:, k * H + 128 * (nt0 + r):k * H + 128 * (nt0 + r + 1)],
                                ptb[:, 128 * kk:128 * (kk + 1)])
                        return last
                    cph("scalar", f_ev)

        for g in range(NG):
            t0 = g * TG
            dph([(shq[:, k * TG:(k + 1) * TG], hqT[128 * k:128 * (k + 1), t0:t0 + TG])
                 for k in range(NI)])
            for mc0 in range(0, NH, 4):
                def f_mm(eng, mc0=mc0):
                    for q in range(4):
                        mc = mc0 + q
                        for k in range(NI):
                            last = eng.matmul(pmm[q][:],
                                              wdq[:, k * H + 128 * mc:k * H + 128 * (mc + 1)],
                                              shq[:, k * TG:(k + 1) * TG],
                                              start=(k == 0), stop=(k == NI - 1))
                    return last
                cph("tensor", f_mm)

                def f_ev(eng, mc0=mc0):
                    for q in range(4):
                        mc = mc0 + q
                        last = eng.copy(sot[:, mc * TG:(mc + 1) * TG], pmm[q][:])
                    return last
                cph("scalar", f_ev)

            dph([(sx[:, j * H:(j + 1) * H], xres[t0 + 128 * j:t0 + 128 * (j + 1), :])
                 for j in range(NTT)])
            for j in range(NTT):
                def f_tp(eng, j=j):
                    for hc in range(NH):
                        last = eng.transpose(pt[:, 128 * hc:128 * (hc + 1)],
                                             sot[:, hc * TG + 128 * j:hc * TG + 128 * (j + 1)],
                                             sidf[:])
                    return last
                cph("tensor", f_tp)
                cph("vector", lambda eng, j=j: eng.scalar_tensor_tensor(
                    sx[:, j * H:(j + 1) * H], pt[:, 0:H], 0.5,
                    sx[:, j * H:(j + 1) * H], ALU.mult, ALU.add))
            dph([(xout[t0 + 128 * j:t0 + 128 * (j + 1), :], sx[:, j * H:(j + 1) * H])
                 for j in range(NTT)])

        sch.emit(nc, block, chain)
    return nc


def run_l1ffn(x_full, ins):
    H, I = 1024, 2816
    identb = np.eye(128).astype(BF_NP)
    identf = np.eye(128, dtype=np.float32)

    nc = cached('build_l1ffn_a', build_l1ffn_a, TLOC)
    maps = []
    for c in range(NCORES):
        b, hh = c // 2, c % 2
        maps.append({"xin": np.ascontiguousarray(x_full[b, hh * TLOC:(hh + 1) * TLOC, :]),
                     "identb": identb})
    res = run_bass_kernel_spmd(nc, maps, core_ids=list(range(NCORES)))
    xqT = [res.results[c]["xqT"] for c in range(NCORES)]

    nc = cached('build_l1ffn_b', build_l1ffn_b, TLOC)
    maps = []
    for c in range(NCORES):
        maps.append({"xqTin": xqT[c], "wg": ins["L1_ffn_w_gate"],
                     "wu": ins["L1_ffn_w_up"], "identb": identb, "identf": identf})
    res = run_bass_kernel_spmd(nc, maps, core_ids=list(range(NCORES)))
    hq = [res.results[c]["hq"] for c in range(NCORES)]

    nc = cached('build_l1ffn_d', build_l1ffn_d, TLOC)
    maps = []
    for c in range(NCORES):
        b, hh = c // 2, c % 2
        maps.append({"hqT": np.ascontiguousarray(hq[c].T), "wd": ins["L1_ffn_w_down"],
                     "xres": np.ascontiguousarray(x_full[b, hh * TLOC:(hh + 1) * TLOC, :]),
                     "identf": identf, "identb": identb})
    res = run_bass_kernel_spmd(nc, maps, core_ids=list(range(NCORES)))
    out = np.empty((BATCH, SEQ, DM), np.float32)
    for c in range(NCORES):
        b, hh = c // 2, c % 2
        out[b, hh * TLOC:(hh + 1) * TLOC, :] = res.results[c]["xout"]
    return out


def build_l1attn_a(ntok):
    nc = bass.Bass("TRN2", target_bir_lowering=False)
    TG = 256
    NG = ntok // TG
    NTT = TG // 128
    H, J = 1024, 4096
    NH, NJ = H // 128, J // 128

    xin = nc.dram_tensor("xin", [ntok, H], F32, kind="ExternalInput").ap()
    WinT = nc.dram_tensor("WinT", [H, J], F32, kind="ExternalInput").ap()
    grep = nc.dram_tensor("grep", [128, H], F32, kind="ExternalInput").ap()
    identf = nc.dram_tensor("identf", [128, 128], F32, kind="ExternalInput").ap()
    ln4out = nc.dram_tensor("ln4out", [ntok, H], F32, kind="ExternalOutput").ap()
    xz = nc.dram_tensor("xz", [J, ntok], F32, kind="ExternalOutput").ap()

    with contextlib.ExitStack() as st:
        def sb(name, shape, dt=F32):
            return st.enter_context(nc.sbuf_tensor(name, shape, dt))
        winbf = sb("winbf", [128, NH * J], BF16)
        stag = sb("stag", [128, J], F32)
        sgrep = sb("sgrep", [128, H], F32)
        sident = sb("sident", [128, 128], F32)
        seps = sb("seps", [128, 1], F32)
        seps6 = sb("seps6", [128, 1], F32)
        sx = sb("sx", [128, H * NTT], F32)
        shid = sb("shid", [128, H * NTT], F32)
        smu = sb("smu", [128, NTT], F32)
        ssq = sb("ssq", [128, NTT], F32)
        srstd = sb("srstd", [128, NTT], F32)
        sscr = sb("sscr", [128, H], F32)
        shTb = sb("shTb", [128, NH * TG], BF16)
        sxz = sb("sxz", [128, 4 * TG], F32)

        pt = st.enter_context(nc.psum_tensor("pt", [128, 512], F32))
        pmm = [st.enter_context(nc.psum_tensor(f"pmm{q}", [128, TG], F32)) for q in range(4)]

        chain = st.enter_context(nc.semaphore("chain"))
        block = st.enter_context(nc.Block())
        sch = Sched()
        dph, gph, cph = mkphases(sch, chain)

        dph([(sgrep[:], grep), (sident[:], identf)])
        cph("vector", lambda eng: eng.memset(seps[:], 1e-5))
        cph("vector", lambda eng: eng.memset(seps6[:], 1e-6))
        for k in range(NH):
            dph([(stag[:], WinT[128 * k:128 * (k + 1), :])])
            cph("vector", lambda eng, k=k: eng.tensor_copy(winbf[:, k * J:(k + 1) * J], stag[:]))

        for g in range(NG):
            t0 = g * TG
            dph([(sx[:, j * H:(j + 1) * H], xin[t0 + 128 * j:t0 + 128 * (j + 1), :])
                 for j in range(NTT)])
            emit_ln(dph, cph, sx, shid, smu, ssq, srstd, sscr, NTT, H, seps[:, 0:1])
            dph([(ln4out[t0 + 128 * j:t0 + 128 * (j + 1), :], shid[:, j * H:(j + 1) * H])
                 for j in range(NTT)])

            def f_sq(eng):
                for j in range(NTT):
                    last = eng.activation(sscr[:, 0:H], shid[:, j * H:(j + 1) * H],
                                          AF.Square, accum_out=ssq[:, j:j + 1])
                return last
            cph("scalar", f_sq)
            cph("vector", lambda eng: eng.tensor_scalar(ssq[:, 0:NTT], ssq[:, 0:NTT],
                                                        1.0 / H, None, ALU.mult))
            cph("scalar", lambda eng: eng.activation(ssq[:, 0:NTT], ssq[:, 0:NTT],
                                                     AF.Sqrt, bias=seps6[0:128, 0:1]))
            cph("vector", lambda eng: eng.reciprocal(ssq[:, 0:NTT], ssq[:, 0:NTT]))

            def f_n(eng):
                for j in range(NTT):
                    last = eng.tensor_scalar(shid[:, j * H:(j + 1) * H],
                                             shid[:, j * H:(j + 1) * H],
                                             ssq[:, j:j + 1], None, ALU.mult)
                return last
            cph("vector", f_n)

            def f_g(eng):
                for j in range(NTT):
                    last = eng.tensor_tensor(shid[:, j * H:(j + 1) * H],
                                             shid[:, j * H:(j + 1) * H], sgrep[:], ALU.mult)
                return last
            cph("vector", f_g)

            for hc in range(NH):
                def f_tp(eng, hc=hc):
                    for j in range(NTT):
                        last = eng.transpose(pt[:, 128 * j:128 * (j + 1)],
                                             shid[:, j * H + 128 * hc:j * H + 128 * (hc + 1)],
                                             sident[:])
                    return last
                cph("tensor", f_tp)
                cph("scalar", lambda eng, hc=hc: eng.copy(
                    shTb[:, hc * TG:(hc + 1) * TG], pt[:, 0:TG]))

            for jc0 in range(0, NJ, 4):
                def f_mm(eng, jc0=jc0):
                    for q in range(4):
                        jc = jc0 + q
                        for k in range(NH):
                            last = eng.matmul(pmm[q][:],
                                              winbf[:, k * J + 128 * jc:k * J + 128 * (jc + 1)],
                                              shTb[:, k * TG:(k + 1) * TG],
                                              start=(k == 0), stop=(k == NH - 1))
                    return last
                cph("tensor", f_mm)

                def f_ev(eng, jc0=jc0):
                    for q in range(4):
                        last = eng.copy(sxz[:, q * TG:(q + 1) * TG], pmm[q][:])
                    return last
                cph("scalar", f_ev)
                dph([(xz[128 * (jc0 + q):128 * (jc0 + q + 1), t0:t0 + TG],
                      sxz[:, q * TG:(q + 1) * TG]) for q in range(4)])

        sch.emit(nc, block, chain)
    return nc


def build_l1attn_b2(ntok, first_half):
    nc = bass.Bass("TRN2", target_bir_lowering=False)
    TGc = 512
    NG = ntok // TGc
    I, P = 2048, 33
    NIc = I // 128
    W = TGc + 3

    xs = nc.dram_tensor("xs", [I, ntok], F32, kind="ExternalInput").ap()
    xsprev = nc.dram_tensor("xsprev", [I, 4], F32, kind="ExternalInput").ap()
    convw = nc.dram_tensor("convw", [I, 4], F32, kind="ExternalInput").ap()
    convb = nc.dram_tensor("convb", [I, 1], F32, kind="ExternalInput").ap()
    WxT = nc.dram_tensor("WxT", [I, P], F32, kind="ExternalInput").ap()
    xc_out = nc.dram_tensor("xc", [I, ntok], F32, kind="ExternalOutput").ap()
    prm_out = nc.dram_tensor("prm", [P, ntok], F32, kind="ExternalOutput").ap()

    with contextlib.ExitStack() as st:
        def sb(name, shape, dt=F32):
            return st.enter_context(nc.sbuf_tensor(name, shape, dt))
        scw = sb("scw", [128, NIc * 4], F32)
        scb = sb("scb", [128, NIc], F32)
        swx = sb("swx", [128, NIc * P], F32)
        swxb = sb("swxb", [128, NIc * P], BF16)
        sxs = sb("sxs", [128, NIc * W], F32)
        sxc = sb("sxc", [128, NIc * TGc], F32)
        sxcb = sb("sxcb", [128, NIc * TGc], BF16)
        sprm = sb("sprm", [33, TGc], F32)

        pp = st.enter_context(nc.psum_tensor("pp", [33, TGc], F32))

        chain = st.enter_context(nc.semaphore("chain"))
        block = st.enter_context(nc.Block())
        sch = Sched()
        dph, gph, cph = mkphases(sch, chain)

        dph([(scw[:, it * 4:(it + 1) * 4], convw[128 * it:128 * (it + 1), :])
             for it in range(NIc)])
        dph([(scb[:, it:it + 1], convb[128 * it:128 * (it + 1), :])
             for it in range(NIc)])
        dph([(swx[:, it * P:(it + 1) * P], WxT[128 * it:128 * (it + 1), :])
             for it in range(NIc)])
        cph("vector", lambda eng: eng.tensor_copy(swxb[:], swx[:]))

        for g in range(NG):
            t0 = g * TGc
            if g == 0:
                if first_half:
                    cph("vector", lambda eng: eng.memset(sxs[:], 0.0))
                else:
                    dph([(sxs[:, it * W:it * W + 3], xsprev[128 * it:128 * (it + 1), 1:4])
                         for it in range(NIc)])
                dph([(sxs[:, it * W + 3:(it + 1) * W],
                      xs[128 * it:128 * (it + 1), t0:t0 + TGc]) for it in range(NIc)])
            else:
                dph([(sxs[:, it * W:(it + 1) * W],
                      xs[128 * it:128 * (it + 1), t0 - 3:t0 + TGc]) for it in range(NIc)])

            def f_c0(eng):
                for it in range(NIc):
                    last = eng.tensor_scalar(sxc[:, it * TGc:(it + 1) * TGc],
                                             sxs[:, it * W:it * W + TGc],
                                             scw[:, it * 4:it * 4 + 1], None, ALU.mult)
                return last
            cph("vector", f_c0)
            for tau in range(1, 4):
                def f_ct(eng, tau=tau):
                    for it in range(NIc):
                        last = eng.scalar_tensor_tensor(
                            sxc[:, it * TGc:(it + 1) * TGc],
                            sxs[:, it * W + tau:it * W + tau + TGc],
                            scw[:, it * 4 + tau:it * 4 + tau + 1],
                            sxc[:, it * TGc:(it + 1) * TGc], ALU.mult, ALU.add)
                    return last
                cph("vector", f_ct)

            def f_cb(eng):
                for it in range(NIc):
                    last = eng.tensor_scalar(sxc[:, it * TGc:(it + 1) * TGc],
                                             sxc[:, it * TGc:(it + 1) * TGc],
                                             scb[:, it:it + 1], None, ALU.add)
                return last
            cph("vector", f_cb)

            def f_silu(eng):
                for it in range(NIc):
                    last = eng.activation(sxc[:, it * TGc:(it + 1) * TGc],
                                          sxc[:, it * TGc:(it + 1) * TGc], AF.Silu)
                return last
            cph("scalar", f_silu)
            dph([(xc_out[128 * it:128 * (it + 1), t0:t0 + TGc],
                  sxc[:, it * TGc:(it + 1) * TGc]) for it in range(NIc)])
            cph("vector", lambda eng: eng.tensor_copy(sxcb[:], sxc[:]))

            def f_mm(eng):
                for k in range(NIc):
                    last = eng.matmul(pp[:], swxb[:, k * P:(k + 1) * P],
                                      sxcb[:, k * TGc:(k + 1) * TGc],
                                      start=(k == 0), stop=(k == NIc - 1))
                return last
            cph("tensor", f_mm)
            cph("scalar", lambda eng: eng.copy(sprm[:], pp[:]))
            dph([(prm_out[:, t0:t0 + TGc], sprm[:])])

        sch.emit(nc, block, chain)
    return nc


def build_l1attn_c(iloc, L):
    nc = bass.Bass("TRN2", target_bir_lowering=False)
    NI = iloc // 128
    D = 16
    TC = 512
    NTC = L // TC

    xc = nc.dram_tensor("xcin", [iloc, L], F32, kind="ExternalInput").ap()
    zin = nc.dram_tensor("zin", [iloc, L], F32, kind="ExternalInput").ap()
    prm = nc.dram_tensor("prm", [33, L], F32, kind="ExternalInput").ap()
    wdt = nc.dram_tensor("wdt", [iloc, 1], F32, kind="ExternalInput").ap()
    bdt = nc.dram_tensor("bdt", [iloc, 1], F32, kind="ExternalInput").ap()
    alog = nc.dram_tensor("alog", [iloc, D], F32, kind="ExternalInput").ap()
    dvec = nc.dram_tensor("dvec", [iloc, 1], F32, kind="ExternalInput").ap()
    onescol128 = nc.dram_tensor("onescol128", [1, 128], F32, kind="ExternalInput").ap()
    sy_out = nc.dram_tensor("syout", [iloc, L], F32, kind="ExternalOutput").ap()

    with contextlib.ExitStack() as st:
        def sb(name, shape, dt=F32):
            return st.enter_context(nc.sbuf_tensor(name, shape, dt))
        swdt = sb("swdt", [128, NI], F32)
        sbdt = sb("sbdt", [128, NI], F32)
        sA = sb("sA", [128, NI * D], F32)
        sD = sb("sD", [128, NI], F32)
        sones1 = sb("sones1", [128, 1], F32)
        sonesc = sb("sonesc", [1, 128], F32)
        sstate = sb("sstate", [128, NI * D], F32)
        sldrow = sb("sldrow", [1, TC], F32)
        sbrow = sb("sbrow", [1, D * TC], F32)
        scrow = sb("scrow", [1, D * TC], F32)
        sldrep = sb("sldrep", [128, TC], F32)
        sbrep = sb("sbrep", [128, D * TC], BF16)
        screp = sb("screp", [128, D * TC], BF16)
        sxc = sb("sxct", [128, NI * TC], F32)
        sdt = sb("sdt", [128, NI * TC], F32)
        sy = sb("syt", [128, NI * TC], F32)
        sa8 = sb("sa8", [128, NI * TC], F32)
        sh = sb("sh", [128, NI * TC], F32)
        swt = sb("swt", [128, NI * TC], BF16)
        su = sb("su", [128, NI * TC], BF16)

        prep = [st.enter_context(nc.psum_tensor(f"prep{q}", [128, TC], F32))
                for q in range(4)]

        chain = st.enter_context(nc.semaphore("chain"))
        block = st.enter_context(nc.Block())
        sch = Sched()
        dph, gph, cph = mkphases(sch, chain)

        dph([(swdt[:, it:it + 1], wdt[128 * it:128 * (it + 1), :]) for it in range(NI)])
        dph([(sbdt[:, it:it + 1], bdt[128 * it:128 * (it + 1), :]) for it in range(NI)])
        dph([(sA[:, it * D:(it + 1) * D], alog[128 * it:128 * (it + 1), :])
             for it in range(NI)])
        dph([(sD[:, it:it + 1], dvec[128 * it:128 * (it + 1), :]) for it in range(NI)])
        dph([(sonesc[:], onescol128)])
        cph("vector", lambda eng: eng.memset(sones1[:], 1.0))
        cph("vector", lambda eng: eng.memset(sstate[:], 0.0))
        cph("scalar", lambda eng: eng.activation(sA[:], sA[:], AF.Exp))
        cph("vector", lambda eng: eng.tensor_scalar(sA[:], sA[:], -1.0, None, ALU.mult))

        for tc in range(NTC):
            c0 = tc * TC
            dph([(sxc[:, it * TC:(it + 1) * TC], xc[128 * it:128 * (it + 1), c0:c0 + TC])
                 for it in range(NI)])
            dph([(sldrow[:], prm[32:33, c0:c0 + TC]),
                 (sbrow[:], prm[0:16, c0:c0 + TC]),
                 (scrow[:], prm[16:32, c0:c0 + TC])])
            cph("tensor", lambda eng: eng.matmul(prep[0][:], sonesc[:], sldrow[:],
                                                 start=True, stop=True))
            cph("scalar", lambda eng: eng.copy(sldrep[:], prep[0][:]))
            for blk, (rows, dstrep) in enumerate(((sbrow, sbrep), (scrow, screp))):
                for d0 in range(0, D, 4):
                    def f_mm(eng, rows=rows, d0=d0):
                        for q in range(4):
                            last = eng.matmul(prep[q][:], sonesc[:],
                                              rows[0:1, (d0 + q) * TC:(d0 + q + 1) * TC],
                                              start=True, stop=True)
                        return last
                    cph("tensor", f_mm)

                    def f_ev(eng, dstrep=dstrep, d0=d0):
                        for q in range(4):
                            last = eng.copy(dstrep[:, (d0 + q) * TC:(d0 + q + 1) * TC],
                                            prep[q][:])
                        return last
                    cph("scalar", f_ev)

            def f_d1(eng):
                for it in range(NI):
                    last = eng.tensor_scalar(sdt[:, it * TC:(it + 1) * TC], sldrep[:],
                                             swdt[:, it:it + 1], None, ALU.mult)
                return last
            cph("vector", f_d1)

            def f_d2(eng):
                for it in range(NI):
                    last = eng.tensor_scalar(sdt[:, it * TC:(it + 1) * TC],
                                             sdt[:, it * TC:(it + 1) * TC],
                                             sbdt[:, it:it + 1], None, ALU.add)
                return last
            cph("vector", f_d2)
            cph("scalar", lambda eng: eng.activation(sdt[:], sdt[:], AF.Exp))
            cph("scalar", lambda eng: eng.activation(sdt[:], sdt[:], AF.Ln,
                                                     bias=sones1[:, 0:1]))
            cph("vector", lambda eng: eng.tensor_tensor(swt[:], sdt[:], sxc[:], ALU.mult))
            cph("vector", lambda eng: eng.memset(sy[:], 0.0))

            for d in range(D):
                def f_a(eng, d=d):
                    for it in range(NI):
                        last = eng.activation(sa8[:, it * TC:(it + 1) * TC],
                                              sdt[:, it * TC:(it + 1) * TC], AF.Exp,
                                              scale=sA[:, it * D + d:it * D + d + 1])
                    return last
                cph("scalar", f_a)

                def f_u(eng, d=d):
                    for it in range(NI):
                        last = eng.tensor_tensor(su[:, it * TC:(it + 1) * TC],
                                                 swt[:, it * TC:(it + 1) * TC],
                                                 sbrep[:, d * TC:(d + 1) * TC], ALU.mult)
                    return last
                cph("vector", f_u)

                def f_scan(eng, d=d):
                    for it in range(NI):
                        last = eng.tensor_tensor_scan(
                            sh[:, it * TC:(it + 1) * TC],
                            sa8[:, it * TC:(it + 1) * TC],
                            su[:, it * TC:(it + 1) * TC],
                            sstate[:, it * D + d:it * D + d + 1],
                            ALU.mult, ALU.add)
                    return last
                cph("vector", f_scan)

                def f_sv(eng, d=d):
                    for it in range(NI):
                        last = eng.tensor_copy(sstate[:, it * D + d:it * D + d + 1],
                                               sh[:, (it + 1) * TC - 1:(it + 1) * TC])
                    return last
                cph("vector", f_sv)

                def f_hc(eng, d=d):
                    for it in range(NI):
                        last = eng.tensor_tensor(su[:, it * TC:(it + 1) * TC],
                                                 sh[:, it * TC:(it + 1) * TC],
                                                 screp[:, d * TC:(d + 1) * TC], ALU.mult)
                    return last
                cph("vector", f_hc)

                def f_acc(eng, d=d):
                    return eng.tensor_tensor(sy[:], sy[:], su[:], ALU.add)
                cph("vector", f_acc)

            def f_xd(eng):
                for it in range(NI):
                    last = eng.tensor_scalar(sh[:, it * TC:(it + 1) * TC],
                                             sxc[:, it * TC:(it + 1) * TC],
                                             sD[:, it:it + 1], None, ALU.mult)
                return last
            cph("vector", f_xd)
            cph("vector", lambda eng: eng.tensor_tensor(sy[:], sy[:], sh[:], ALU.add))
            dph([(sa8[:, it * TC:(it + 1) * TC], zin[128 * it:128 * (it + 1), c0:c0 + TC])
                 for it in range(NI)])
            cph("scalar", lambda eng: eng.activation(sa8[:], sa8[:], AF.Silu))
            cph("vector", lambda eng: eng.tensor_tensor(sy[:], sy[:], sa8[:], ALU.mult))
            dph([(sy_out[128 * it:128 * (it + 1), c0:c0 + TC],
                  sy[:, it * TC:(it + 1) * TC]) for it in range(NI)])

        sch.emit(nc, block, chain)
    return nc


def build_l1attn_d(ntok):
    nc = bass.Bass("TRN2", target_bir_lowering=False)
    TG = 256
    NG = ntok // TG
    NTT = TG // 128
    H, I = 1024, 2048
    NH, NIc = H // 128, I // 128

    syin = nc.dram_tensor("syin", [I, ntok], F32, kind="ExternalInput").ap()
    WoutT = nc.dram_tensor("WoutT", [I, H], F32, kind="ExternalInput").ap()
    x4in = nc.dram_tensor("x4in", [ntok, H], F32, kind="ExternalInput").ap()
    ln4in = nc.dram_tensor("ln4in", [ntok, H], F32, kind="ExternalInput").ap()
    identf = nc.dram_tensor("identf", [128, 128], F32, kind="ExternalInput").ap()
    x5out = nc.dram_tensor("x5out", [ntok, H], F32, kind="ExternalOutput").ap()

    with contextlib.ExitStack() as st:
        def sb(name, shape, dt=F32):
            return st.enter_context(nc.sbuf_tensor(name, shape, dt))
        woutbf = sb("woutbf", [128, NIc * H], BF16)
        stag = sb("stag", [128, H], F32)
        sident = sb("sident", [128, 128], F32)
        ssy = sb("ssy", [128, NIc * TG], F32)
        ssyb = sb("ssyb", [128, NIc * TG], BF16)
        sot = sb("sot", [128, NH * TG], F32)
        sx = sb("sx", [128, H * NTT], F32)
        sl = sb("sl", [128, H * NTT], F32)

        pt = st.enter_context(nc.psum_tensor("pt", [128, 1024], F32))
        pmm = [st.enter_context(nc.psum_tensor(f"pmm{q}", [128, TG], F32)) for q in range(4)]

        chain = st.enter_context(nc.semaphore("chain"))
        block = st.enter_context(nc.Block())
        sch = Sched()
        dph, gph, cph = mkphases(sch, chain)

        dph([(sident[:], identf)])
        for k in range(NIc):
            dph([(stag[:], WoutT[128 * k:128 * (k + 1), :])])
            cph("vector", lambda eng, k=k: eng.tensor_copy(woutbf[:, k * H:(k + 1) * H], stag[:]))

        for g in range(NG):
            t0 = g * TG
            dph([(ssy[:, k * TG:(k + 1) * TG], syin[128 * k:128 * (k + 1), t0:t0 + TG])
                 for k in range(NIc)])
            cph("vector", lambda eng: eng.tensor_copy(ssyb[:], ssy[:]))
            for mc0 in range(0, NH, 4):
                def f_mm(eng, mc0=mc0):
                    for q in range(4):
                        mc = mc0 + q
                        for k in range(NIc):
                            last = eng.matmul(pmm[q][:],
                                              woutbf[:, k * H + 128 * mc:k * H + 128 * (mc + 1)],
                                              ssyb[:, k * TG:(k + 1) * TG],
                                              start=(k == 0), stop=(k == NIc - 1))
                    return last
                cph("tensor", f_mm)

                def f_ev(eng, mc0=mc0):
                    for q in range(4):
                        mc = mc0 + q
                        last = eng.copy(sot[:, mc * TG:(mc + 1) * TG], pmm[q][:])
                    return last
                cph("scalar", f_ev)

            dph([(sx[:, j * H:(j + 1) * H], x4in[t0 + 128 * j:t0 + 128 * (j + 1), :])
                 for j in range(NTT)])
            dph([(sl[:, j * H:(j + 1) * H], ln4in[t0 + 128 * j:t0 + 128 * (j + 1), :])
                 for j in range(NTT)])
            for j in range(NTT):
                def f_tp(eng, j=j):
                    for hc in range(NH):
                        last = eng.transpose(pt[:, 128 * hc:128 * (hc + 1)],
                                             sot[:, hc * TG + 128 * j:hc * TG + 128 * (j + 1)],
                                             sident[:])
                    return last
                cph("tensor", f_tp)
                cph("vector", lambda eng, j=j: eng.tensor_tensor(
                    sx[:, j * H:(j + 1) * H], sx[:, j * H:(j + 1) * H],
                    sl[:, j * H:(j + 1) * H], ALU.add))
                cph("vector", lambda eng, j=j: eng.tensor_tensor(
                    sx[:, j * H:(j + 1) * H], sx[:, j * H:(j + 1) * H],
                    pt[:, 0:H], ALU.add))
            dph([(x5out[t0 + 128 * j:t0 + 128 * (j + 1), :], sx[:, j * H:(j + 1) * H])
                 for j in range(NTT)])

        sch.emit(nc, block, chain)
    return nc


def run_l1attn(x4_full, ins):
    H, J, I = 1024, 4096, 2048
    identf = np.eye(128, dtype=np.float32)

    nc = cached('build_l1attn_a', build_l1attn_a, TLOC)
    common = {
        "WinT": np.ascontiguousarray(ins["L1_attn_W_in"].T),
        "grep": np.ascontiguousarray(np.broadcast_to(ins["L1_attn_g"], (128, H))),
        "identf": identf,
    }
    maps = []
    for c in range(NCORES):
        b, hh = c // 2, c % 2
        m = dict(common)
        m["xin"] = np.ascontiguousarray(x4_full[b, hh * TLOC:(hh + 1) * TLOC, :])
        maps.append(m)
    res = run_bass_kernel_spmd(nc, maps, core_ids=list(range(NCORES)))
    ln4 = np.empty((BATCH, SEQ, H), np.float32)
    xz = np.empty((BATCH, J, SEQ), np.float32)
    for c in range(NCORES):
        b, hh = c // 2, c % 2
        ln4[b, hh * TLOC:(hh + 1) * TLOC, :] = res.results[c]["ln4out"]
        xz[b, :, hh * TLOC:(hh + 1) * TLOC] = res.results[c]["xz"]

    nc0 = cached('l1b2', build_l1attn_b2, TLOC, True)
    nc1 = cached('l1b2', build_l1attn_b2, TLOC, False)
    common = {
        "convw": ins["L1_attn_conv_w"],
        "convb": ins["L1_attn_conv_b"].reshape(I, 1),
        "WxT": np.ascontiguousarray(ins["L1_attn_W_x"].T),
    }
    maps0, maps1 = [], []
    for c in range(NCORES):
        b, hh = c // 2, c % 2
        m = dict(common)
        m["xs"] = np.ascontiguousarray(xz[b, :I, hh * TLOC:(hh + 1) * TLOC])
        m["xsprev"] = np.ascontiguousarray(xz[b, :I, TLOC - 4:TLOC]) if hh == 1 else \
            np.zeros((I, 4), np.float32)
        (maps1 if hh else maps0).append(m)
    res0 = run_bass_kernel_spmd(nc0, maps0, core_ids=[0, 2, 4, 6])
    res1 = run_bass_kernel_spmd(nc1, maps1, core_ids=[0, 2, 4, 6])
    xc = np.empty((BATCH, I, SEQ), np.float32)
    prm = np.empty((BATCH, 33, SEQ), np.float32)
    for b in range(BATCH):
        xc[b, :, 0:TLOC] = res0.results[b]["xc"]
        xc[b, :, TLOC:] = res1.results[b]["xc"]
        prm[b, :, 0:TLOC] = res0.results[b]["prm"]
        prm[b, :, TLOC:] = res1.results[b]["prm"]

    iloc = I // 2
    nc = cached('l1c', build_l1attn_c, iloc, SEQ)
    maps = []
    for c in range(NCORES):
        b, ih = c // 2, c % 2
        sl = slice(ih * iloc, (ih + 1) * iloc)
        maps.append({
            "xcin": np.ascontiguousarray(xc[b, sl, :]),
            "zin": np.ascontiguousarray(xz[b, I + sl.start:I + sl.stop, :]),
            "prm": np.ascontiguousarray(prm[b]),
            "wdt": np.ascontiguousarray(ins["L1_attn_W_dt"][sl]),
            "bdt": np.ascontiguousarray(ins["L1_attn_b_dt"][sl].reshape(iloc, 1)),
            "alog": np.ascontiguousarray(ins["L1_attn_A_log"][sl]),
            "dvec": np.ascontiguousarray(ins["L1_attn_D"][sl].reshape(iloc, 1)),
            "onescol128": np.ones((1, 128), np.float32),
        })
    res = run_bass_kernel_spmd(nc, maps, core_ids=list(range(NCORES)))
    sy = np.empty((BATCH, I, SEQ), np.float32)
    for c in range(NCORES):
        b, ih = c // 2, c % 2
        sy[b, ih * iloc:(ih + 1) * iloc, :] = res.results[c]["syout"]

    nc = cached('build_l1attn_d', build_l1attn_d, TLOC)
    common = {"WoutT": np.ascontiguousarray(ins["L1_attn_W_out"].T), "identf": identf}
    maps = []
    for c in range(NCORES):
        b, hh = c // 2, c % 2
        m = dict(common)
        m["syin"] = np.ascontiguousarray(sy[b, :, hh * TLOC:(hh + 1) * TLOC])
        m["x4in"] = np.ascontiguousarray(x4_full[b, hh * TLOC:(hh + 1) * TLOC, :])
        m["ln4in"] = np.ascontiguousarray(ln4[b, hh * TLOC:(hh + 1) * TLOC, :])
        maps.append(m)
    res = run_bass_kernel_spmd(nc, maps, core_ids=list(range(NCORES)))
    x5 = np.empty((BATCH, SEQ, DM), np.float32)
    for c in range(NCORES):
        b, hh = c // 2, c % 2
        x5[b, hh * TLOC:(hh + 1) * TLOC, :] = res.results[c]["x5out"]
    return x5, xz, xc, prm, sy


def kernel(**inputs):
    x0 = np.ascontiguousarray(inputs["L0_attn_x"], dtype=np.float32)
    x1, _ = run_l0ffn(x0, inputs)
    x2, _, _ = run_l0attn(x1, inputs)
    x3, _ = run_l0ffn(x2, inputs)
    x4 = run_l1ffn(x3, inputs)
    x5, _, _, _, _ = run_l1attn(x4, inputs)
    x6 = run_l1ffn(x5, inputs)
    return x6

